# version 12
# baseline (speedup 1.0000x reference)
import os
os.environ.setdefault("NEURON_SCRATCHPAD_PAGE_SIZE", "512")
import numpy as np
import ml_dtypes
import concourse.bass as bass
import concourse.mybir as mybir
from concourse.bass_utils import run_bass_kernel_spmd
from contextlib import ExitStack

F32 = mybir.dt.float32
BF16 = mybir.dt.bfloat16
I32 = mybir.dt.int32
U8 = mybir.dt.uint8
AF = mybir.ActivationFunctionType
ALU = mybir.AluOpType

ARO_INV = [0, 2, 4, 6, 1, 3, 5, 7]
NCORES = 8
D = 4096
TOK = 1024
QTS = 512
NDC = 32
ICH = 86
LI = 12
SCALE_QK = float(1.0 / np.sqrt(128.0))
EPS = 1e-5
ROWB = 16384
FFNQ = None
PASS2 = True


class Sink:
    def __getattr__(self, name):
        return lambda *a, **k: None


class C:
    def __init__(self, kind, eng, sems):
        self.k = kind
        self.e = eng
        self.sems = sems
        self.n = {s: 0 for s in sems}
        sink = Sink()
        self.t = eng if kind == 't' else sink
        self.v = eng if kind == 'v' else sink
        self.a = eng if kind == 'a' else sink
        self.g = eng if kind == 'g' else sink
        self.s = eng if kind == 's' else sink
        self.roff = None
        self.trace = []

    SEM_OWNER = {'T': 't', 'V': 'v', 'A': 'a', 'g': 'g', 'w': 's', 'x': 'a', 'c': 'g',
                 'x0': 'a', 'x1': 'a', 'w0': 's', 'w1': 's', 'u0': 's', 'u1': 's',
                 'd0': 's', 'd1': 's', 'd2': 's', 'd3': 's',
                 'kv0': 'a', 'kv1': 'a', 'y0': 'a', 'y1': 'a',
                 'q0': 's', 'q1': 's', 'pc': 'g'}

    def inc(self, instr, sem, amt=1):
        self.n[sem] += amt
        self.trace.append((self.SEM_OWNER[sem], 'inc', sem, amt))
        if instr is not None:
            instr.then_inc(self.sems[sem], amt)
        return self.n[sem]

    def w(self, kind, sem, val):
        assert kind in ('t', 'v', 'a', 'g', 's')
        if val <= 0:
            return
        assert val <= self.n[sem], f"wait {sem}>={val} but only {self.n[sem]} emitted"
        self.trace.append((kind, 'wait', sem, val))
        if self.k == kind:
            self.e.wait_ge(self.sems[sem], val)

    DMA_SEMS = ['g', 'w', 'x', 'x0', 'x1', 'w0', 'w1', 'u0', 'u1',
                'd0', 'd1', 'd2', 'd3', 'kv0', 'kv1', 'y0', 'y1', 'q0', 'q1', 'pc']

    def dma(self, q, out, in_, sem=None):
        eng = {'g': self.g, 'w': self.s, 'x': self.a}[q]
        i = eng.dma_start(out=out, in_=in_)
        if i is None and self.k == {'g': 'g', 'w': 's', 'x': 'a'}[q]:
            raise RuntimeError("dma emit failed")
        return self.inc(i, sem or q, 16)

    def wait_all_dma(self, kind):
        for s in self.DMA_SEMS:
            self.w(kind, s, self.n[s])


def _simulate_deadlock(trace):
    qs = {e: [] for e in 'tvags'}
    for ev in trace:
        qs[ev[0]].append(ev)
    heads = {e: 0 for e in 'tvags'}
    sems = {}
    progress = True
    while progress:
        progress = False
        for e in 'tvags':
            while heads[e] < len(qs[e]):
                ev = qs[e][heads[e]]
                if ev[1] == 'inc':
                    sems[ev[2]] = sems.get(ev[2], 0) + ev[3]
                    heads[e] += 1
                    progress = True
                elif sems.get(ev[2], 0) >= ev[3]:
                    heads[e] += 1
                    progress = True
                else:
                    break
    stuck = [(e, qs[e][heads[e]]) for e in 'tvags' if heads[e] < len(qs[e])]
    if stuck:
        for e, ev in stuck:
            print(f"DEADLOCK: engine {e} stuck at {ev}, sem={dict(sems)}"[:200])
        raise AssertionError(f"deadlock: {stuck[:3]}")


def build(n_sub=4, dump_deq=False, ffn_mode='full'):
    assert not dump_deq
    nc = bass.Bass("TRN2", target_bir_lowering=False, num_devices=NCORES)

    xT = nc.dram_tensor("xT", [D, TOK], F32, kind="ExternalInput")
    wq_a = nc.dram_tensor("wq_a", [D, 4096], F32, kind="ExternalInput")
    wk_a = nc.dram_tensor("wk_a", [D, 1024], F32, kind="ExternalInput")
    wv_a = nc.dram_tensor("wv_a", [D, 1024], F32, kind="ExternalInput")
    wo_a = nc.dram_tensor("wo_a", [D, 4096], F32, kind="ExternalInput")
    wq_b = nc.dram_tensor("wq_b", [D, 4096], F32, kind="ExternalInput")
    wk_b = nc.dram_tensor("wk_b", [D, 1024], F32, kind="ExternalInput")
    wv_b = nc.dram_tensor("wv_b", [D, 1024], F32, kind="ExternalInput")
    wo_b = nc.dram_tensor("wo_b", [D, 4096], F32, kind="ExternalInput")
    gqb = nc.dram_tensor("gqb", [D, 768], U8, kind="ExternalInput")
    uqb = nc.dram_tensor("uqb", [D, 768], U8, kind="ExternalInput")
    dqb = nc.dram_tensor("dqb", [1536, 2048], U8, kind="ExternalInput")
    gsc = nc.dram_tensor("gsc", [32, 1536], F32, kind="ExternalInput")
    usc = nc.dram_tensor("usc", [32, 1536], F32, kind="ExternalInput")
    dsc = nc.dram_tensor("dsc", [16, 4096], F32, kind="ExternalInput")
    gzb = nc.dram_tensor("gzb", [32, 768], U8, kind="ExternalInput")
    uzb = nc.dram_tensor("uzb", [32, 768], U8, kind="ExternalInput")
    dzb = nc.dram_tensor("dzb", [16, 2048], U8, kind="ExternalInput")
    cosI = nc.dram_tensor("cosI", [64, TOK], F32, kind="ExternalInput")
    sinI = nc.dram_tensor("sinI", [64, TOK], F32, kind="ExternalInput")
    maskI = nc.dram_tensor("maskI", [2, 16, 128, QTS], BF16, kind="ExternalInput")
    onesI = nc.dram_tensor("onesI", [128, 128], BF16, kind="ExternalInput")
    rbI = nc.dram_tensor("rbI", [1, 4], I32, kind="ExternalInput")

    yT = nc.dram_tensor("yT", [D, TOK], F32, kind="ExternalOutput")


    xstate = nc.dram_tensor("xstate", [D, TOK], F32)
    bq_a = nc.dram_tensor("bq_a", [D, 4096], BF16)
    bo_a = nc.dram_tensor("bo_a", [D, 4096], BF16)
    bq_b = nc.dram_tensor("bq_b", [D, 4096], BF16)
    bk_b = nc.dram_tensor("bk_b", [D, 1024], BF16)
    bv_b = nc.dram_tensor("bv_b", [D, 1024], BF16)
    bo_b = nc.dram_tensor("bo_b", [D, 4096], BF16)
    kvp = nc.dram_tensor("kvp", [2048, 1024], BF16)
    kvfull = nc.dram_tensor("kvfull", [8 * 2048, 1024], BF16)
    kvpair = nc.dram_tensor("kvpair", [2 * 2048, 1024], BF16)
    deqp = nc.dram_tensor("deqp", [36 * 128, 4096], BF16)
    deqfull = nc.dram_tensor("deqfull", [8 * 36 * 128, 4096], BF16)
    zs_g = nc.dram_tensor("zs_g", [32, 1536], F32)
    zs_u = nc.dram_tensor("zs_u", [32, 1536], F32)
    zs_d = nc.dram_tensor("zs_d", [16, 4096], F32)

    with ExitStack() as ctx:
        en = ctx.enter_context
        ones = en(nc.sbuf_tensor("ones", [128, 128], BF16))
        cosT = en(nc.sbuf_tensor("cosT", [64, TOK], F32))
        sinT = en(nc.sbuf_tensor("sinT", [64, TOK], F32))
        xnA = en(nc.sbuf_tensor("xnA", [128, ROWB], BF16))
        KTA = en(nc.sbuf_tensor("KTA", [128, ROWB], BF16))
        VTA = en(nc.sbuf_tensor("VTA", [128, ROWB], BF16))
        QTA = en(nc.sbuf_tensor("QTA", [128, ROWB], BF16))
        CTA = en(nc.sbuf_tensor("CTA", [128, ROWB], BF16))
        wbuf = en(nc.sbuf_tensor("wbuf", [128, 8192], BF16))
        xstg = en(nc.sbuf_tensor("xstg", [128, 1024], F32))
        yout = en(nc.sbuf_tensor("yout", [128, 1024], F32))
        ebuf = en(nc.sbuf_tensor("ebuf", [128, 1024], BF16))
        m_sb = en(nc.sbuf_tensor("m_sb", [128, 512], F32))
        rs_sb = en(nc.sbuf_tensor("rs_sb", [128, 512], F32))
        var_sb = en(nc.sbuf_tensor("var_sb", [128, 512], F32))
        tmp_sb = en(nc.sbuf_tensor("tmp_sb", [128, 512], F32))
        rb_sb = en(nc.sbuf_tensor("rb_sb", [1, 4], I32))

        bt_u8s = [VTA[:, 0:1024].bitcast(U8),
                  VTA[:, 13312:14336].bitcast(U8)]
        bf_f = VTA[:, 1024:5120].bitcast(F32)
        tt_f = VTA[:, 5120:9216].bitcast(F32)
        hi_i = VTA[:, 9216:13312].bitcast(I32)
        lo_f = CTA[:, 4096:8192].bitcast(F32)
        sb_f = CTA[:, 8192:16384].bitcast(F32)
        zsb_f = KTA[:, 0:8192].bitcast(F32)
        Wt_b = KTA[:, 8192:12288]
        zt_f = KTA[:, 12288:16384].bitcast(F32)
        rtv = CTA[:, 0:4096].bitcast(F32)
        sqst = ebuf
        sgst = ebuf
        zrow_z = zt_f[0:32, 0:768]
        zrow_s = zt_f[0:32, 768:1536]

        pb = [en(nc.psum_tensor(f"pb{i}", [128, 512], F32)) for i in range(8)]

        sems = {s: en(nc.semaphore("sem_" + s)) for s in
                ['g', 'w', 'x', 'c', 'T', 'V', 'A',
                 'x0', 'x1', 'w0', 'w1', 'u0', 'u1',
                 'd0', 'd1', 'd2', 'd3', 'kv0', 'kv1', 'y0', 'y1',
                 'q0', 'q1', 'pc']}

        block = en(nc.Block())

        def dap(h, off, dims):
            return bass.AP(h, off, dims)

        def wslot(par):
            return bass.AP(wbuf, par * 4096, [[8192, 128], [128, 32], [1, 128]])

        def plo(ap_):
            return bass.AP(ap_.tensor, ap_.offset, [[ap_.ap[0][0], 64], list(ap_.ap[1])])

        def phi(ap_):
            return bass.AP(ap_.tensor, ap_.offset + 64 * ap_.ap[0][0],
                           [[ap_.ap[0][0], 64], list(ap_.ap[1])])

        def slab(view, m4, nW, np_):
            return bass.AP(view.tensor, view.offset + m4,
                           [[view.ap[0][0], np_], [4, nW]])

        def outslab(view, p8, nW, np_):
            return bass.AP(view.tensor, view.offset + ARO_INV[p8],
                           [[view.ap[0][0], np_], [8, nW]])

        def xchunk(src, dc, c0):
            return dap(src, dc * 128 * TOK + c0, [[TOK, 128], [1, QTS]])

        def emit_ln(C, src, c0):
            C.w('a', 'T', C.n['T'])
            C.w('a', 'V', C.n['V'])
            for s_ in ('x0', 'x1', 'kv0', 'kv1'):
                C.w('a', s_, C.n[s_])
            t0 = C.n['T']
            a0 = C.n['A']
            lmk = [C.dma('x', xstg[:, 0:512], xchunk(src, 0, c0), sem='x0')]
            for dc in range(NDC):
                par = dc % 2
                if dc + 1 < NDC:
                    s2 = 'x' + str((dc + 1) % 2)
                    lmk.append(C.dma('x', xstg[:, ((dc + 1) % 2) * 512:((dc + 1) % 2 + 1) * 512],
                                     xchunk(src, dc + 1, c0), sem=s2))
                C.w('a', 'x' + str(par), lmk[dc])
                i1 = C.a.activation(xnA[:, dc * 512:(dc + 1) * 512],
                                    xstg[:, par * 512:(par + 1) * 512], AF.Copy)
                C.inc(i1, 'A')
                if dc >= 2:
                    C.w('a', 'T', t0 + dc - 1)
                i2 = C.a.activation(sqst[:, par * 512:(par + 1) * 512],
                                    xstg[:, par * 512:(par + 1) * 512], AF.Square)
                C.inc(i2, 'A')
                C.w('t', 'A', a0 + 2 * (dc + 1))
                C.t.matmul(pb[6][:, :], ones[:, :], xnA[:, dc * 512:(dc + 1) * 512],
                           start=(dc == 0), stop=(dc == NDC - 1), skip_group_check=True)
                i3 = C.t.matmul(pb[7][:, :], ones[:, :], sqst[:, par * 512:(par + 1) * 512],
                                start=(dc == 0), stop=(dc == NDC - 1), skip_group_check=True)
                C.inc(i3, 'T')
            C.w('v', 'T', t0 + NDC)
            C.v.tensor_scalar(m_sb[:, :], pb[6][:, :], 1.0 / D, None, ALU.mult)
            C.v.tensor_scalar(var_sb[:, :], pb[7][:, :], 1.0 / D, None, ALU.mult)
            C.v.tensor_tensor(tmp_sb[:, :], m_sb[:, :], m_sb[:, :], ALU.mult)
            C.v.tensor_tensor(var_sb[:, :], var_sb[:, :], tmp_sb[:, :], ALU.subtract)
            iv = C.v.tensor_scalar(var_sb[:, :], var_sb[:, :], EPS, None, ALU.add)
            v1 = C.inc(iv, 'V')
            C.w('a', 'V', v1)
            ia = C.a.activation(tmp_sb[:, :], var_sb[:, :], AF.Sqrt)
            a1 = C.inc(ia, 'A')
            C.w('v', 'A', a1)
            ivr = C.v.reciprocal(rs_sb[:, :], tmp_sb[:, :])
            C.inc(ivr, 'V')
            xm = []
            lmk2 = [C.dma('x', xstg[:, 0:512], xchunk(src, 0, c0), sem='x0')]
            for dc in range(NDC):
                par = dc % 2
                if dc + 1 < NDC:
                    if dc >= 1:
                        C.w('a', 'V', xm[dc - 1])
                    s2 = 'x' + str((dc + 1) % 2)
                    lmk2.append(C.dma('x', xstg[:, ((dc + 1) % 2) * 512:((dc + 1) % 2 + 1) * 512],
                                      xchunk(src, dc + 1, c0), sem=s2))
                C.w('v', 'x' + str(par), lmk2[dc])
                if PASS2:
                    C.v.tensor_tensor(var_sb[:, :], xstg[:, par * 512:(par + 1) * 512],
                                      m_sb[:, :], ALU.subtract)
                else:
                    C.v.tensor_tensor(var_sb[:, :], xnA[:, dc * 512:(dc + 1) * 512],
                                      m_sb[:, :], ALU.subtract)
                iv = C.v.tensor_tensor(xnA[:, dc * 512:(dc + 1) * 512], var_sb[:, :],
                                       rs_sb[:, :], ALU.mult)
                xm.append(C.inc(iv, 'V'))
            return C.n['V']

        def emit_rope(C, psum, out_bf, c0):
            cs = cosT[:, c0:c0 + 512]
            sn = sinT[:, c0:c0 + 512]
            lo_ps = psum[0:64, :]
            hi_ps = psum[64:128, :]
            ra = rtv[0:64, 0:512]
            rb = rtv[0:64, 512:1024]
            C.v.tensor_tensor(ra, lo_ps, cs, ALU.mult)
            C.v.tensor_tensor(rb, hi_ps, sn, ALU.mult)
            C.v.tensor_tensor(plo(out_bf), ra, rb, ALU.subtract)
            C.v.tensor_tensor(ra, hi_ps, cs, ALU.mult)
            C.v.tensor_tensor(rb, lo_ps, sn, ALU.mult)
            return C.v.tensor_tensor(phi(out_bf), ra, rb, ALU.add)

        def emit_dequant_prologue(C):
            for (srcw, dstw) in ((wq_a, bq_a), (wo_a, bo_a)):
                C.dma('g', dstw.ap(), srcw.ap(), sem='pc')
            C.dma('x', ones[:, :], onesI.ap(), sem='y0')
            C.dma('x', cosT[:, :], cosI.ap(), sem='y0')
            C.dma('x', sinT[:, :], sinI.ap(), sem='y0')
            C.dma('x', rb_sb[:, :], rbI.ap(), sem='y0')
            C.w('t', 'y0', C.n['y0'])
            C.w('v', 'y0', C.n['y0'])

            def unpack_stage(np_, nbytes):
                C.v.tensor_scalar(tt_f[0:np_, 0:nbytes], bf_f[0:np_, 0:nbytes], 0.0625,
                                  0.46875, ALU.mult, ALU.subtract)
                C.v.tensor_copy(hi_i[0:np_, 0:nbytes], tt_f[0:np_, 0:nbytes])
                C.v.scalar_tensor_tensor(lo_f[0:np_, 0:nbytes], hi_i[0:np_, 0:nbytes],
                                         -16.0, bf_f[0:np_, 0:nbytes], ALU.mult, ALU.add)

            chunks = []
            cst = {'i': 0, 'marks': []}

            def bt_next():
                ci = cst['i']
                cst['i'] += 1
                par = ci % 2
                if ci >= 2:
                    C.w('g', 'V', cst['marks'][ci - 2])
                return par

            def stage_z(first, half, zbI_, scI_, zsD):
                c0 = half * 768
                par = bt_next()
                bt_u8 = bt_u8s[par]
                gl = C.dma('g', bt_u8[0:32, 0:384], dap(zbI_, half * 384, [[768, 32], [1, 384]]),
                           sem='d' + str(par))
                xl = C.dma('x', zrow_s, dap(scI_, c0, [[1536, 32], [1, 768]]))
                C.w('v', 'd' + str(par), gl)
                if not first:
                    C.w('v', 'x', C.n['x'])
                ic1 = C.v.tensor_copy(bf_f[0:32, 0:384], bt_u8[0:32, 0:384])
                cst['marks'].append(C.inc(ic1, 'V'))
                unpack_stage(32, 384)
                for m4 in range(4):
                    for lohi in range(2):
                        p8 = 2 * m4 + lohi
                        srct = lo_f if lohi == 0 else hi_i
                        C.v.tensor_copy(outslab(zrow_z, p8, 96, 32), slab(srct, m4, 96, 32))
                C.w('v', 'x', xl)
                iv = C.v.tensor_tensor(zrow_z, zrow_z, zrow_s, ALU.mult)
                vz = C.inc(iv, 'V')
                C.w('a', 'V', vz)
                C.dma('x', dap(zsD, c0, [[1536, 32], [1, 768]]), zrow_z)

            for zi_, zpk in enumerate(((gzb, gsc, zs_g), (uzb, usc, zs_u))):
                for h_ in range(2):
                    chunks.append(lambda f_=(zi_ == 0 and h_ == 0), h_=h_, zpk=zpk:
                                  stage_z(f_, h_, *zpk))

            def down_zeros():
                par = bt_next()
                bt_u8 = bt_u8s[par]
                gl = C.dma('g', bt_u8[0:16, 0:2048], dzb.ap(), sem='d' + str(par))
                C.w('v', 'd' + str(par), gl)
                C.w('v', 'x', C.n['x'])
                ic1 = C.v.tensor_copy(bf_f[0:16, 0:2048], bt_u8[0:16, 0:2048])
                cst['marks'].append(C.inc(ic1, 'V'))
                unpack_stage(16, 2048)
                for m4 in range(4):
                    for lohi in range(2):
                        p8 = 2 * m4 + lohi
                        srct = lo_f if lohi == 0 else hi_i
                        C.v.tensor_copy(outslab(zsb_f[0:16, 0:4096], p8, 512, 16),
                                        slab(srct, m4, 512, 16))
                xl = C.dma('x', sb_f[0:16, 0:4096], dsc.ap())
                C.w('v', 'x', xl)
                iv = C.v.tensor_tensor(sb_f[0:16, 0:4096], zsb_f[0:16, 0:4096],
                                       sb_f[0:16, 0:4096], ALU.mult)
                vz = C.inc(iv, 'V')
                C.w('a', 'V', vz)
                C.zsd_mark = C.dma('x', zs_d.ap(), sb_f[0:16, 0:4096])

            chunks.append(down_zeros)

            def gu_chunk(mi, dc, qbI_, scI_, zsD):
                if mi == 0 and dc == 0:
                    C.w('a', 'x', C.zsd_mark)
                par = bt_next()
                bt_u8 = bt_u8s[par]
                gl = C.dma('g', bt_u8[:, 0:768],
                           dap(qbI_, dc * 128 * 768, [[768, 128], [1, 768]]),
                           sem='d' + str(par))
                C.w('a', 'V', C.n['V'])
                x1 = C.dma('x', sb_f[:, 0:1536], dap(scI_, dc * 1536, [[0, 128], [1, 1536]]))
                x2 = C.dma('x', zsb_f[:, 0:1536], dap(zsD, dc * 1536, [[0, 128], [1, 1536]]))
                C.w('v', 'd' + str(par), gl)
                C.w('v', 'x', x2)
                C.w('v', 'w', C.n['w'])
                ic1 = C.v.tensor_copy(bf_f[:, 0:768], bt_u8[:, 0:768])
                cst['marks'].append(C.inc(ic1, 'V'))
                unpack_stage(128, 768)
                ivv = None
                for m4 in range(4):
                    for lohi in range(2):
                        p8 = 2 * m4 + lohi
                        srct = lo_f if lohi == 0 else hi_i
                        C.v.tensor_tensor(tt_f[:, 0:192], slab(srct, m4, 192, 128),
                                          outslab(sb_f[:, 0:1536], p8, 192, 128), ALU.mult)
                        ivv = C.v.tensor_tensor(outslab(Wt_b[:, 0:1536], p8, 192, 128),
                                                tt_f[:, 0:192],
                                                outslab(zsb_f[:, 0:1536], p8, 192, 128),
                                                ALU.subtract)
                vW = C.inc(ivv, 'V')
                C.w('s', 'V', vW)
                C.dma('w',
                      dap(deqp, (mi * 12) * 128 * 4096 + dc * 128,
                          [[4096, 128], [128 * 4096, 12], [1, 128]]),
                      bass.AP(Wt_b.tensor, Wt_b.offset,
                              [[ROWB, 128], [128, 12], [1, 128]]))

            for mi_, gpk in enumerate(((gqb, gsc, zs_g), (uqb, usc, zs_u))):
                for dc_ in range(NDC):
                    chunks.append(lambda mi_=mi_, dc_=dc_, gpk=gpk: gu_chunk(mi_, dc_, *gpk))

            def down_chunk(l):
                par = bt_next()
                bt_u8 = bt_u8s[par]
                gl = C.dma('g', bt_u8[:, 0:2048],
                           dap(dqb, l * 128 * 2048, [[2048, 128], [1, 2048]]),
                           sem='d' + str(par))
                C.w('a', 'V', C.n['V'])
                x1 = C.dma('x', sb_f[:, 0:4096], dap(dsc, l * 4096, [[0, 128], [1, 4096]]))
                x2 = C.dma('x', zsb_f[:, 0:4096], dap(zs_d, l * 4096, [[0, 128], [1, 4096]]))
                C.w('v', 'd' + str(par), gl)
                C.w('v', 'x', x2)
                C.w('v', 'w', C.n['w'])
                ic1 = C.v.tensor_copy(bf_f[:, 0:2048], bt_u8[:, 0:2048])
                cst['marks'].append(C.inc(ic1, 'V'))
                unpack_stage(128, 2048)
                ivv = None
                for m4 in range(4):
                    for lohi in range(2):
                        p8 = 2 * m4 + lohi
                        srct = lo_f if lohi == 0 else hi_i
                        C.v.tensor_tensor(tt_f[:, 0:512], slab(srct, m4, 512, 128),
                                          outslab(sb_f[:, 0:4096], p8, 512, 128), ALU.mult)
                        ivv = C.v.tensor_tensor(outslab(Wt_b[:, 0:4096], p8, 512, 128),
                                                tt_f[:, 0:512],
                                                outslab(zsb_f[:, 0:4096], p8, 512, 128),
                                                ALU.subtract)
                vW = C.inc(ivv, 'V')
                C.w('s', 'V', vW)
                C.dma('w', dap(deqp, (24 + l) * 128 * 4096, [[4096, 128], [1, 4096]]),
                      Wt_b[:, 0:4096])

            for l_ in range(LI):
                chunks.append(lambda l_=l_: down_chunk(l_))

            def fire_deq_ag():
                C.wait_all_dma('g')
                icc = C.g.collective_compute(
                    "AllGather", ALU.bypass, replica_groups=[list(range(NCORES))],
                    ins=[deqp.ap().opt()], outs=[deqfull.ap().opt()])
                return C.inc(icc, 'c')
            return chunks, fire_deq_ag

        def emit_attn(C, is_a, src, dst, fire_after_kv=None, deq_chunks=None):
            wq_, wk_, wv_, wo_ = (bq_a, wk_a, wv_a, bo_a) if is_a else \
                                 (bq_b, bk_b, bv_b, bo_b)
            dq = deq_chunks if deq_chunks is not None else []

            def drain(nch):
                while nch > 0 and dq:
                    dq.pop(0)()
                    nch -= 1
            if not is_a:
                C.w('s', 'pc', 6 * 16)
            ev = 'V' if is_a else 'A'
            kvp_marks = []
            for tile in range(2):
                vln = emit_ln(C, src, tile * QTS)
                kq_loads = []
                kq_marks = []

                def issue_k(kvc):
                    p_ = kvc % 2
                    ke = 'g' if is_a else 's'
                    if kvc >= 2:
                        C.w(ke, 'T', kq_marks[kvc - 2])
                    else:
                        C.w(ke, 'T', C.n['T'])
                    kq_loads.append(
                        C.dma('g' if is_a else 'w', wslot(p_),
                              dap(wk_, kvc * 128, [[1024, 128], [128 * 1024, 32], [1, 128]]),
                              sem='q' + str(p_)))

                issue_k(0)
                for kvc in range(8):
                    par = kvc % 2
                    if kvc + 1 < 8:
                        issue_k(kvc + 1)
                    C.w('t', 'q' + str(par), kq_loads[kvc])
                    C.w('t', 'V', vln)
                    C.w('t', ev, C.n[ev])
                    it = None
                    for dc in range(NDC):
                        it = C.t.matmul(pb[4 + par][:, :],
                                        wbuf[:, par * 4096 + dc * 128:par * 4096 + (dc + 1) * 128],
                                        xnA[:, dc * 512:(dc + 1) * 512],
                                        start=(dc == 0), stop=(dc == NDC - 1))
                    tK = C.inc(it, 'T')
                    kq_marks.append(tK)
                    eb = ebuf[:, par * 512:(par + 1) * 512]
                    if len(kvp_marks) >= 2:
                        C.w('v' if is_a else 'a', 'kv' + str(par), kvp_marks[-2])
                    if is_a:
                        C.w('v', 'T', tK)
                        iv = emit_rope(C, pb[4 + par], eb, tile * QTS)
                        emark = C.inc(iv, 'V')
                    else:
                        C.w('a', 'T', tK)
                        ia = C.a.activation(eb, pb[4 + par][:, :], AF.Copy)
                        emark = C.inc(ia, 'A')
                    C.w('a', ev, emark)
                    kvp_marks.append(
                        C.dma('x', dap(kvp, kvc * 128 * 1024 + tile * QTS,
                                       [[1024, 128], [1, QTS]]), eb, sem='kv' + str(par)))
                    drain(3)
                for kvh in range(2):
                    C.w('g' if is_a else 's', 'T', C.n['T'])
                    gl = C.dma('g' if is_a else 'w', bass.AP(QTA, 0, [[ROWB, 128], [512, 32], [1, 512]]),
                               dap(wv_, kvh * 512, [[1024, 128], [128 * 1024, 32], [1, 512]]),
                               sem='q0')
                    for tc in range(4):
                        par = tc % 2
                        C.w('t', 'q0', gl)
                        C.w('t', 'A', C.n['A'])
                        it = None
                        for dc in range(NDC):
                            it = C.t.matmul(pb[4 + par][:, :],
                                            xnA[:, dc * 512 + tc * 128:dc * 512 + (tc + 1) * 128],
                                            QTA[:, dc * 512:(dc + 1) * 512],
                                            start=(dc == 0), stop=(dc == NDC - 1))
                        tV = C.inc(it, 'T')
                        C.w('a', 'T', tV)
                        eb = ebuf[:, par * 512:(par + 1) * 512]
                        if len(kvp_marks) >= 2:
                            C.w('a', 'kv' + str(par), kvp_marks[-2])
                        ia = C.a.activation(eb, pb[4 + par][:, :], AF.Copy)
                        aV = C.inc(ia, 'A')
                        kvp_marks.append(
                            C.dma('x', dap(kvp, (1024 + tile * QTS + tc * 128) * 1024 + kvh * 512,
                                           [[1024, 128], [1, 512]]), eb, sem='kv' + str(par)))
                        drain(2)
            drain(10 ** 9)
            C.wait_all_dma('g')
            icc = C.g.collective_compute(
                "AllGather", ALU.bypass, replica_groups=[list(range(NCORES))],
                ins=[kvp.ap().opt()], outs=[kvfull.ap().opt()])
            ccv = C.inc(icc, 'c')
            C.w('g', 'c', ccv)
            C.w('g', 'T', C.n['T'])
            C.w('g', 'V', C.n['V'])
            for j, dstrow in enumerate((0, 2048, 1024, 3072)):
                if C.k == 'g':
                    C.roff = C.e.value_load(rb_sb[0:1, j:j + 1])
                    i = C.e.dma_start(out=dap(kvpair, dstrow * 1024, [[1024, 1024], [1, 1024]]),
                                      in_=kvfull.ap()[bass.ds(C.roff, 1024), :])
                else:
                    i = None
                C.inc(i, 'kv0' if j < 2 else 'kv1', 16)
            C.w('g', 'kv0', C.n['kv0'])
            C.w('g', 'kv1', C.n['kv1'])
            for g8 in range(8):
                for half in range(2):
                    C.dma('g',
                          KTA[:, g8 * 2048 + half * 1024:g8 * 2048 + (half + 1) * 1024],
                          dap(kvpair, (half * 2048 + g8 * 128) * 1024,
                              [[1024, 128], [1, 1024]]))
                    C.dma('g',
                          bass.AP(VTA, g8 * 2048 + half * 1024,
                                  [[ROWB, 128], [128, 8], [1, 128]]),
                          dap(kvpair, (half * 2048 + 1024) * 1024 + g8 * 128,
                              [[1024, 128], [128 * 1024, 8], [1, 128]]))
            kv_ready = C.n['g']
            if fire_after_kv is not None:
                C.deq_ready = fire_after_kv()

            if is_a:
                C.w('s', 'pc', 2 * 16)
            for qt in range(2):
                vln = emit_ln(C, src, qt * QTS)
                qp_loads = []
                qp_marks = []

                def issue_q(oc):
                    p_ = oc % 2
                    if oc >= 2:
                        C.w('s', 'T', qp_marks[oc - 2])
                    else:
                        C.w('s', 'T', C.n['T'])
                    qp_loads.append(
                        C.dma('w', wslot(p_),
                              dap(wq_, oc * 128, [[4096, 128], [128 * 4096, 32], [1, 128]]),
                              sem='q' + str(p_)))

                issue_q(0)
                for oc in range(NDC):
                    par = oc % 2
                    if oc + 1 < NDC:
                        issue_q(oc + 1)
                    C.w('t', 'q' + str(par), qp_loads[oc])
                    C.w('t', 'V', vln)
                    C.w('t', ev, C.n[ev])
                    it = None
                    for dc in range(NDC):
                        it = C.t.matmul(pb[4 + par][:, :],
                                        wbuf[:, par * 4096 + dc * 128:par * 4096 + (dc + 1) * 128],
                                        xnA[:, dc * 512:(dc + 1) * 512],
                                        start=(dc == 0), stop=(dc == NDC - 1))
                    tQ = C.inc(it, 'T')
                    qp_marks.append(tQ)
                    qsl = QTA[:, oc * 512:(oc + 1) * 512]
                    if is_a:
                        C.w('v', 'T', tQ)
                        iv = emit_rope(C, pb[4 + par], qsl, qt * QTS)
                        C.inc(iv, 'V')
                    else:
                        C.w('a', 'T', tQ)
                        ia = C.a.activation(qsl, pb[4 + par][:, :], AF.Copy)
                        C.inc(ia, 'A')
                tq_done = C.n['T']
                if is_a:
                    C.w('a', 'T', tq_done)
                    C.w('a', 'V', C.n['V'])
                    mload = C.dma('x', bass.AP(xnA, 8192, [[ROWB, 128], [512, 16], [1, 512]]),
                                  dap(maskI, qt * 16 * 128 * 512,
                                      [[512, 128], [128 * 512, 16], [1, 512]]))
                else:
                    mload = 0
                C.w('t', 'g', kv_ready)
                nkc = 12 if (is_a and qt == 0) else 16
                for head in range(32):
                    g8 = head // 4
                    a_exp0 = C.n['A']
                    v_msk0 = C.n['V']
                    for kc in range(nkc):
                        par = kc % 2
                        C.w('t', 'A', a_exp0 + kc - 1 if kc >= 2 else a_exp0)
                        it = C.t.matmul(pb[par][:, :],
                                        KTA[:, g8 * 2048 + kc * 128:g8 * 2048 + (kc + 1) * 128],
                                        QTA[:, head * 512:(head + 1) * 512],
                                        start=True, stop=True)
                        ts_ = C.inc(it, 'T')
                        C.w('a', 'T', ts_)
                        ia = C.a.activation(xnA[:, kc * 512:(kc + 1) * 512], pb[par][:, :],
                                            AF.Exp, scale=SCALE_QK)
                        C.inc(ia, 'A')
                        if is_a:
                            C.w('v', 'A', C.n['A'])
                            if head == 0 and kc == 0:
                                C.w('v', 'x', mload)
                            iv = C.v.tensor_tensor(xnA[:, kc * 512:(kc + 1) * 512],
                                                   xnA[:, kc * 512:(kc + 1) * 512],
                                                   xnA[:, 8192 + kc * 512:8192 + (kc + 1) * 512],
                                                   ALU.mult)
                            C.inc(iv, 'V')
                    if is_a:
                        C.w('t', 'V', v_msk0 + nkc)
                    else:
                        C.w('t', 'A', a_exp0 + nkc)
                    C.w('t', 'V', C.n['V'])
                    it = None
                    for kc in range(nkc):
                        it = C.t.matmul(pb[2][:, :], ones[:, :],
                                        xnA[:, kc * 512:(kc + 1) * 512],
                                        start=(kc == 0), stop=(kc == nkc - 1))
                    tsum = C.inc(it, 'T')
                    C.w('v', 'T', tsum)
                    iv = C.v.reciprocal(rs_sb[:, :], pb[2][:, :])
                    C.inc(iv, 'V')
                    it = None
                    for kc in range(nkc):
                        it = C.t.matmul(pb[3][:, :],
                                        VTA[:, g8 * 2048 + kc * 128:g8 * 2048 + (kc + 1) * 128],
                                        xnA[:, kc * 512:(kc + 1) * 512],
                                        start=(kc == 0), stop=(kc == nkc - 1))
                    tctx = C.inc(it, 'T')
                    C.w('v', 'T', tctx)
                    iv = C.v.tensor_tensor(CTA[:, head * 512:(head + 1) * 512], pb[3][:, :],
                                           rs_sb[:, :], ALU.mult)
                    C.inc(iv, 'V')
                vctx_done = C.n['V']
                res_marks = []
                wo_loads = []
                wo_marks = []

                def issue_wo(hc):
                    p_ = hc % 2
                    if hc >= 2:
                        C.w('s', 'T', wo_marks[hc - 2])
                    else:
                        C.w('s', 'T', C.n['T'])
                    wo_loads.append(
                        C.dma('w', wslot(p_),
                              dap(wo_, hc * 128, [[4096, 128], [128 * 4096, 32], [1, 128]]),
                              sem='q' + str(p_)))

                issue_wo(0)
                for hc in range(NDC):
                    par = hc % 2
                    if hc + 1 < NDC:
                        issue_wo(hc + 1)
                    if len(res_marks) >= 2:
                        C.w('a', 'V', res_marks[-2][0])
                    xl = C.dma('x', xstg[:, par * 512:(par + 1) * 512],
                               xchunk(src, hc, qt * QTS), sem='x' + str(par))
                    C.w('t', 'q' + str(par), wo_loads[hc])
                    C.w('t', 'V', vctx_done)
                    C.w('t', 'V', C.n['V'])
                    it = None
                    for oc in range(NDC):
                        it = C.t.matmul(pb[4 + par][:, :],
                                        wbuf[:, par * 4096 + oc * 128:par * 4096 + (oc + 1) * 128],
                                        CTA[:, oc * 512:(oc + 1) * 512],
                                        start=(oc == 0), stop=(oc == NDC - 1))
                    tWo = C.inc(it, 'T')
                    wo_marks.append(tWo)
                    C.w('v', 'T', tWo)
                    C.w('v', 'x' + str(par), xl)
                    if len(res_marks) >= 2:
                        C.w('v', 'y' + str(par), res_marks[-2][1])
                    iv = C.v.tensor_tensor(yout[:, par * 512:(par + 1) * 512], pb[4 + par][:, :],
                                           xstg[:, par * 512:(par + 1) * 512], ALU.add)
                    vres = C.inc(iv, 'V')
                    C.w('a', 'V', vres)
                    wm = C.dma('x', xchunk(dst, hc, qt * QTS),
                               yout[:, par * 512:(par + 1) * 512], sem='y' + str(par))
                    res_marks.append((vres, wm))
            if is_a:
                for (srcw, dstw) in ((wq_b, bq_b), (wk_b, bk_b), (wv_b, bv_b), (wo_b, bo_b)):
                    C.dma('g', dstw.ap(), srcw.ap(), sem='pc')

        def emit_ffn(C, src, dst, deq_ready):
            def h_view(ic):
                if ic < 32:
                    return QTA[:, ic * 512:(ic + 1) * 512]
                if ic < 64:
                    return CTA[:, (ic - 32) * 512:(ic - 31) * 512]
                return KTA[:, (ic - 64) * 512:(ic - 63) * 512]

            def gblk(mi, ic):
                r_, l_ = ic // LI, ic % LI
                return (r_ * 36 + mi * 12 + l_) * 128 * 4096

            C.w('s', 'c', deq_ready)
            for tt in range(2):
                vln = emit_ln(C, src, tt * QTS)
                gu_loads = []
                gu_marks = []

                def issue_gu(ic):
                    par = ic % 2
                    if ic >= 2:
                        C.w('s', 'T', gu_marks[ic - 2])
                    else:
                        C.w('s', 'T', C.n['T'])
                        C.w('s', 'V', C.n['V'])
                    C.dma('w', wbuf[:, par * 4096:(par + 1) * 4096],
                          dap(deqfull, gblk(0, ic), [[4096, 128], [1, 4096]]), sem='w' + str(par))
                    gu_loads.append(
                        (C.n['w' + str(par)],
                         C.dma('w', VTA[:, par * 4096:(par + 1) * 4096],
                               dap(deqfull, gblk(1, ic), [[4096, 128], [1, 4096]]),
                               sem='u' + str(par))))

                issue_gu(0)
                for ic in range(ICH):
                    par = ic % 2
                    if ic + 1 < ICH:
                        issue_gu(ic + 1)
                    C.w('t', 'w' + str(par), gu_loads[ic][0])
                    C.w('t', 'u' + str(par), gu_loads[ic][1])
                    C.w('t', 'V', vln)
                    C.w('t', 'A', C.n['A'])
                    C.w('t', 'V', C.n['V'])
                    for dc in range(NDC):
                        C.t.matmul(pb[0][:, :],
                                   wbuf[:, par * 4096 + dc * 128:par * 4096 + (dc + 1) * 128],
                                   xnA[:, dc * 512:(dc + 1) * 512],
                                   start=(dc == 0), stop=(dc == NDC - 1))
                    it = None
                    for dc in range(NDC):
                        it = C.t.matmul(pb[1][:, :],
                                        VTA[:, par * 4096 + dc * 128:par * 4096 + (dc + 1) * 128],
                                        xnA[:, dc * 512:(dc + 1) * 512],
                                        start=(dc == 0), stop=(dc == NDC - 1))
                    tg = C.inc(it, 'T')
                    gu_marks.append(tg)
                    C.w('a', 'T', tg)
                    ia = C.a.activation(sgst[:, par * 512:(par + 1) * 512], pb[0][:, :], AF.Silu)
                    C.inc(ia, 'A')
                    C.w('v', 'A', C.n['A'])
                    iv = C.v.tensor_tensor(h_view(ic), sgst[:, par * 512:(par + 1) * 512],
                                           pb[1][:, :], ALU.mult)
                    C.inc(iv, 'V')
                h_done = C.n['V']
                if ffn_mode == 'nodown':
                    res_marks = []
                    for hc in range(NDC):
                        par = hc % 2
                        if len(res_marks) >= 2:
                            C.w('a', 'V', res_marks[-2][0])
                        xl = C.dma('x', xstg[:, par * 512:(par + 1) * 512],
                                   xchunk(src, hc, tt * QTS), sem='x' + str(par))
                        C.w('v', 'x' + str(par), xl)
                        if len(res_marks) >= 2:
                            C.w('v', 'y' + str(par), res_marks[-2][1])
                        iv = C.v.tensor_scalar(yout[:, par * 512:(par + 1) * 512],
                                               xstg[:, par * 512:(par + 1) * 512], 1.0, None,
                                               ALU.mult)
                        vres = C.inc(iv, 'V')
                        C.w('a', 'V', vres)
                        wm = C.dma('x', xchunk(dst, hc, tt * QTS),
                                   yout[:, par * 512:(par + 1) * 512], sem='y' + str(par))
                        res_marks.append((vres, wm))
                    continue
                d_loads = []
                d_marks = []

                def issue_dn(k):
                    dcp_, ic_ = k // ICH, k % ICH
                    sl = k % 4
                    if k >= 4:
                        C.w('s', 'T', d_marks[k - 4])
                    r_, l_ = ic_ // LI, ic_ % LI
                    d_loads.append(
                        C.dma('w', VTA[:, 8192 + sl * 1024:8192 + (sl + 1) * 1024],
                              dap(deqfull, (r_ * 36 + 24 + l_) * 128 * 4096 + dcp_ * 1024,
                                  [[4096, 128], [1, 1024]]), sem='d' + str(sl)))

                issue_dn(0)
                issue_dn(1)
                for dcp in range(4):
                    for ic in range(ICH):
                        k = dcp * ICH + ic
                        sl = k % 4
                        if k + 2 < 4 * ICH:
                            issue_dn(k + 2)
                        C.w('t', 'd' + str(sl), d_loads[k])
                        if k == 0:
                            C.w('t', 'V', h_done)
                        it = None
                        for j in range(8):
                            it = C.t.matmul(
                                pb[j][:, :],
                                VTA[:, 8192 + sl * 1024 + j * 128:8192 + sl * 1024 + (j + 1) * 128],
                                h_view(ic),
                                start=(ic == 0), stop=(ic == ICH - 1), skip_group_check=True)
                        d_marks.append(C.inc(it, 'T'))
                    tdown = C.n['T']
                    res_marks = []
                    for j in range(8):
                        par = j % 2
                        hc = dcp * 8 + j
                        if len(res_marks) >= 2:
                            C.w('a', 'V', res_marks[-2][0])
                        xl = C.dma('x', xstg[:, par * 512:(par + 1) * 512],
                                   xchunk(src, hc, tt * QTS), sem='x' + str(par))
                        C.w('v', 'T', tdown)
                        C.w('v', 'x' + str(par), xl)
                        if len(res_marks) >= 2:
                            C.w('v', 'y' + str(par), res_marks[-2][1])
                        iv = C.v.tensor_tensor(yout[:, par * 512:(par + 1) * 512],
                                               pb[j][:, :],
                                               xstg[:, par * 512:(par + 1) * 512], ALU.add)
                        vres = C.inc(iv, 'V')
                        C.w('a', 'V', vres)
                        wm = C.dma('x', xchunk(dst, hc, tt * QTS),
                                   yout[:, par * 512:(par + 1) * 512], sem='y' + str(par))
                        res_marks.append((vres, wm))
                    C.w('t', 'V', C.n['V'])

        def program(C):
            deq_chunks, fire_deq_ag = emit_dequant_prologue(C)
            C.deq_ready = None
            kinds = ['attn_a', 'ffn', 'attn_b', 'ffn']
            for sub in range(n_sub):
                src = xT if sub == 0 else xstate
                dst = yT if sub == n_sub - 1 else xstate
                k = kinds[sub]
                if k == 'ffn':
                    if C.deq_ready is None:
                        C.deq_ready = fire_deq_ag()
                    emit_ffn(C, src, dst, C.deq_ready)
                else:
                    emit_attn(C, k == 'attn_a', src, dst,
                              fire_deq_ag if C.deq_ready is None else None,
                              deq_chunks=deq_chunks if sub == 0 else None)
            if n_sub == 0:
                for dc in range(NDC):
                    for hh in range(2):
                        l = C.dma('g', xstg[:, hh * 512:(hh + 1) * 512],
                                  xchunk(xT, dc, hh * 512))
                        C.w('g', 'g', l)
                        C.dma('g', xchunk(yT, dc, hh * 512), xstg[:, hh * 512:(hh + 1) * 512])
            C.wait_all_dma('g')

        counts = {}

        def run_pass(kind, eng):
            c = C(kind, eng, sems)
            program(c)
            counts[kind] = dict(c.n)
            counts['__trace__'] = c.trace

        block.tensor(lambda e: run_pass('t', e))
        block.vector(lambda e: run_pass('v', e))
        block.scalar(lambda e: run_pass('a', e))
        block.gpsimd(lambda e: run_pass('g', e))
        block.sync(lambda e: run_pass('s', e))

        base = counts['t']
        for k2, v2 in counts.items():
            if k2 == '__trace__':
                continue
            assert v2 == base, f"sem count mismatch {k2}: {v2} vs {base}"
        _simulate_deadlock(counts['__trace__'])

    return nc


def _host_inputs(ins):
    x = np.asarray(ins["L0_attn_a_hidden_states"], np.float32)
    wq_a = np.ascontiguousarray(np.asarray(ins["L0_attn_a_Wq"], np.float32).T)
    wk_a = np.ascontiguousarray(np.asarray(ins["L0_attn_a_Wk"], np.float32).T)
    wv_a = np.ascontiguousarray(np.asarray(ins["L0_attn_a_Wv"], np.float32).T)
    wo_a = np.ascontiguousarray(np.asarray(ins["L0_attn_a_Wo"], np.float32).T)
    wq_b = np.ascontiguousarray(np.asarray(ins["L0_attn_b_wq"], np.float32))
    wk_b = np.ascontiguousarray(np.asarray(ins["L0_attn_b_wk"], np.float32))
    wv_b = np.ascontiguousarray(np.asarray(ins["L0_attn_b_wv"], np.float32))
    wo_b = np.ascontiguousarray(np.asarray(ins["L0_attn_b_wo"], np.float32))

    def padw(a, cols):
        out = np.zeros((a.shape[0], cols), a.dtype)
        out[:, :a.shape[1]] = a
        return out

    gq_p = padw(np.asarray(ins["L0_ffn_gate_qweight"], np.int32), 1536)
    uq_p = padw(np.asarray(ins["L0_ffn_up_qweight"], np.int32), 1536)
    gz_p = padw(np.asarray(ins["L0_ffn_gate_qzeros"], np.int32), 1536)
    uz_p = padw(np.asarray(ins["L0_ffn_up_qzeros"], np.int32), 1536)
    gs_p = padw(np.asarray(ins["L0_ffn_gate_scales"], np.float32), 12288)
    us_p = padw(np.asarray(ins["L0_ffn_up_scales"], np.float32), 12288)
    dq_p = np.zeros((12288, 512), np.int32)
    dq_p[:11008] = np.asarray(ins["L0_ffn_down_qweight"], np.int32)
    dz_p = np.zeros((128, 512), np.int32)
    dz_p[:86] = np.asarray(ins["L0_ffn_down_qzeros"], np.int32)
    ds_p = np.zeros((128, 4096), np.float32)
    ds_p[:86] = np.asarray(ins["L0_ffn_down_scales"], np.float32)

    pos = np.arange(2048, dtype=np.float32)
    freqs = np.exp(-np.arange(0, 128, 2, dtype=np.float32) * (np.log(10000.0) / 128.0))
    ang = pos[:, None] * freqs[None, :]
    cosf = np.ascontiguousarray(np.cos(ang).astype(np.float32).T)
    sinf = np.ascontiguousarray(np.sin(ang).astype(np.float32).T)
    onesc = np.ones((128, 128), dtype=ml_dtypes.bfloat16)

    in_maps = []
    for c in range(NCORES):
        b, h = c // 2, c % 2
        p0 = 1024 * h
        xTc = np.ascontiguousarray(x[b, p0:p0 + 1024, :].T)
        masks = np.zeros((2, 16, 128, 512), dtype=ml_dtypes.bfloat16)
        for qt in range(2):
            qpos = p0 + qt * 512 + np.arange(512)
            for kc in range(16):
                kpos = kc * 128 + np.arange(128)
                masks[qt, kc] = (kpos[:, None] <= qpos[None, :]).astype(ml_dtypes.bfloat16)
        r = c
        dz16 = np.zeros((16, 512), np.int32)
        dz16[:12] = dz_p[12 * r:12 * r + 12]
        ds16 = np.zeros((16, 4096), np.float32)
        ds16[:12] = ds_p[12 * r:12 * r + 12]
        in_maps.append({
            "xT": xTc,
            "wq_a": wq_a, "wk_a": wk_a, "wv_a": wv_a, "wo_a": wo_a,
            "wq_b": wq_b, "wk_b": wk_b, "wv_b": wv_b, "wo_b": wo_b,
            "gqb": np.ascontiguousarray(gq_p[:, 192 * r:192 * (r + 1)]).view(np.uint8),
            "uqb": np.ascontiguousarray(uq_p[:, 192 * r:192 * (r + 1)]).view(np.uint8),
            "dqb": np.ascontiguousarray(dq_p[1536 * r:1536 * (r + 1), :]).view(np.uint8),
            "gsc": np.ascontiguousarray(gs_p[:, 1536 * r:1536 * (r + 1)]),
            "usc": np.ascontiguousarray(us_p[:, 1536 * r:1536 * (r + 1)]),
            "dsc": ds16,
            "gzb": np.ascontiguousarray(gz_p[:, 192 * r:192 * (r + 1)]).view(np.uint8),
            "uzb": np.ascontiguousarray(uz_p[:, 192 * r:192 * (r + 1)]).view(np.uint8),
            "dzb": dz16.view(np.uint8),
            "cosI": np.ascontiguousarray(cosf[:, p0:p0 + 1024]),
            "sinI": np.ascontiguousarray(sinf[:, p0:p0 + 1024]),
            "maskI": masks,
            "onesI": onesc,
            "rbI": np.array([[(c & ~1) * 2048 + off for off in (0, 2048, 1024, 3072)]],
                            np.int32),
        })
    return in_maps


def kernel_run(n_sub=4, dump_deq=False, ffn_mode='full', **inputs):
    in_maps = _host_inputs(inputs)
    nc = build(n_sub, dump_deq, ffn_mode)
    res = run_bass_kernel_spmd(nc, in_maps, core_ids=list(range(NCORES)))
    out = np.zeros((4, 2048, 4096), np.float32)
    for c in range(NCORES):
        b, h = c // 2, c % 2
        out[b, 1024 * h:1024 * (h + 1), :] = res.results[c]["yT"].T
    return out


def kernel(**inputs):
    return kernel_run(4, **inputs)


# version 13 (final)
# speedup vs baseline: 1.0644x; 1.0644x over previous
import os
os.environ.setdefault("NEURON_SCRATCHPAD_PAGE_SIZE", "512")
import numpy as np
import ml_dtypes
import concourse.bass as bass
import concourse.mybir as mybir
from concourse.bass_utils import run_bass_kernel_spmd
from contextlib import ExitStack

F32 = mybir.dt.float32
BF16 = mybir.dt.bfloat16
I32 = mybir.dt.int32
U8 = mybir.dt.uint8
AF = mybir.ActivationFunctionType
ALU = mybir.AluOpType

ARO_INV = [0, 2, 4, 6, 1, 3, 5, 7]
NCORES = 8
D = 4096
TOK = 1024
QTS = 512
NDC = 32
ICH = 86
LI = 12
SCALE_QK = float(1.0 / np.sqrt(128.0))
EPS = 1e-5
ROWB = 16384
FFNQ = None
PASS2 = False


class Sink:
    def __getattr__(self, name):
        return lambda *a, **k: None


class C:
    def __init__(self, kind, eng, sems):
        self.k = kind
        self.e = eng
        self.sems = sems
        self.n = {s: 0 for s in sems}
        sink = Sink()
        self.t = eng if kind == 't' else sink
        self.v = eng if kind == 'v' else sink
        self.a = eng if kind == 'a' else sink
        self.g = eng if kind == 'g' else sink
        self.s = eng if kind == 's' else sink
        self.roff = None
        self.trace = []

    SEM_OWNER = {'T': 't', 'V': 'v', 'A': 'a', 'g': 'g', 'w': 's', 'x': 'a', 'c': 'g',
                 'x0': 'a', 'x1': 'a', 'w0': 's', 'w1': 's', 'u0': 's', 'u1': 's',
                 'd0': 's', 'd1': 's', 'd2': 's', 'd3': 's',
                 'kv0': 'a', 'kv1': 'a', 'y0': 'a', 'y1': 'a',
                 'q0': 's', 'q1': 's', 'pc': 'g'}

    def inc(self, instr, sem, amt=1):
        self.n[sem] += amt
        self.trace.append((self.SEM_OWNER[sem], 'inc', sem, amt))
        if instr is not None:
            instr.then_inc(self.sems[sem], amt)
        return self.n[sem]

    def w(self, kind, sem, val):
        assert kind in ('t', 'v', 'a', 'g', 's')
        if val <= 0:
            return
        assert val <= self.n[sem], f"wait {sem}>={val} but only {self.n[sem]} emitted"
        self.trace.append((kind, 'wait', sem, val))
        if self.k == kind:
            self.e.wait_ge(self.sems[sem], val)

    DMA_SEMS = ['g', 'w', 'x', 'x0', 'x1', 'w0', 'w1', 'u0', 'u1',
                'd0', 'd1', 'd2', 'd3', 'kv0', 'kv1', 'y0', 'y1', 'q0', 'q1', 'pc']

    def dma(self, q, out, in_, sem=None):
        eng = {'g': self.g, 'w': self.s, 'x': self.a}[q]
        i = eng.dma_start(out=out, in_=in_)
        if i is None and self.k == {'g': 'g', 'w': 's', 'x': 'a'}[q]:
            raise RuntimeError("dma emit failed")
        return self.inc(i, sem or q, 16)

    def wait_all_dma(self, kind):
        for s in self.DMA_SEMS:
            self.w(kind, s, self.n[s])


def _simulate_deadlock(trace):
    qs = {e: [] for e in 'tvags'}
    for ev in trace:
        qs[ev[0]].append(ev)
    heads = {e: 0 for e in 'tvags'}
    sems = {}
    progress = True
    while progress:
        progress = False
        for e in 'tvags':
            while heads[e] < len(qs[e]):
                ev = qs[e][heads[e]]
                if ev[1] == 'inc':
                    sems[ev[2]] = sems.get(ev[2], 0) + ev[3]
                    heads[e] += 1
                    progress = True
                elif sems.get(ev[2], 0) >= ev[3]:
                    heads[e] += 1
                    progress = True
                else:
                    break
    stuck = [(e, qs[e][heads[e]]) for e in 'tvags' if heads[e] < len(qs[e])]
    if stuck:
        for e, ev in stuck:
            print(f"DEADLOCK: engine {e} stuck at {ev}, sem={dict(sems)}"[:200])
        raise AssertionError(f"deadlock: {stuck[:3]}")


def build(n_sub=4, dump_deq=False, ffn_mode='full'):
    assert not dump_deq
    nc = bass.Bass("TRN2", target_bir_lowering=False, num_devices=NCORES)

    xT = nc.dram_tensor("xT", [D, TOK], F32, kind="ExternalInput")
    wq_a = nc.dram_tensor("wq_a", [D, 4096], F32, kind="ExternalInput")
    wk_a = nc.dram_tensor("wk_a", [D, 1024], F32, kind="ExternalInput")
    wv_a = nc.dram_tensor("wv_a", [D, 1024], F32, kind="ExternalInput")
    wo_a = nc.dram_tensor("wo_a", [D, 4096], F32, kind="ExternalInput")
    wq_b = nc.dram_tensor("wq_b", [D, 4096], F32, kind="ExternalInput")
    wk_b = nc.dram_tensor("wk_b", [D, 1024], F32, kind="ExternalInput")
    wv_b = nc.dram_tensor("wv_b", [D, 1024], F32, kind="ExternalInput")
    wo_b = nc.dram_tensor("wo_b", [D, 4096], F32, kind="ExternalInput")
    gqb = nc.dram_tensor("gqb", [D, 768], U8, kind="ExternalInput")
    uqb = nc.dram_tensor("uqb", [D, 768], U8, kind="ExternalInput")
    dqb = nc.dram_tensor("dqb", [1536, 2048], U8, kind="ExternalInput")
    gsc = nc.dram_tensor("gsc", [32, 1536], F32, kind="ExternalInput")
    usc = nc.dram_tensor("usc", [32, 1536], F32, kind="ExternalInput")
    dsc = nc.dram_tensor("dsc", [16, 4096], F32, kind="ExternalInput")
    gzb = nc.dram_tensor("gzb", [32, 768], U8, kind="ExternalInput")
    uzb = nc.dram_tensor("uzb", [32, 768], U8, kind="ExternalInput")
    dzb = nc.dram_tensor("dzb", [16, 2048], U8, kind="ExternalInput")
    cosI = nc.dram_tensor("cosI", [64, TOK], F32, kind="ExternalInput")
    sinI = nc.dram_tensor("sinI", [64, TOK], F32, kind="ExternalInput")
    maskI = nc.dram_tensor("maskI", [2, 16, 128, QTS], BF16, kind="ExternalInput")
    onesI = nc.dram_tensor("onesI", [128, 128], BF16, kind="ExternalInput")
    rbI = nc.dram_tensor("rbI", [1, 4], I32, kind="ExternalInput")

    yT = nc.dram_tensor("yT", [D, TOK], F32, kind="ExternalOutput")


    xstate = nc.dram_tensor("xstate", [D, TOK], F32)
    bq_a = nc.dram_tensor("bq_a", [D, 4096], BF16)
    bo_a = nc.dram_tensor("bo_a", [D, 4096], BF16)
    bq_b = nc.dram_tensor("bq_b", [D, 4096], BF16)
    bk_b = nc.dram_tensor("bk_b", [D, 1024], BF16)
    bv_b = nc.dram_tensor("bv_b", [D, 1024], BF16)
    bo_b = nc.dram_tensor("bo_b", [D, 4096], BF16)
    kvp = nc.dram_tensor("kvp", [2048, 1024], BF16)
    kvfull = nc.dram_tensor("kvfull", [8 * 2048, 1024], BF16, addr_space="Shared")
    kvpair = nc.dram_tensor("kvpair", [2 * 2048, 1024], BF16)
    deqp = nc.dram_tensor("deqp", [36 * 128, 4096], BF16)
    deqfull = nc.dram_tensor("deqfull", [8 * 36 * 128, 4096], BF16, addr_space="Shared")
    zs_g = nc.dram_tensor("zs_g", [32, 1536], F32)
    zs_u = nc.dram_tensor("zs_u", [32, 1536], F32)
    zs_d = nc.dram_tensor("zs_d", [16, 4096], F32)

    with ExitStack() as ctx:
        en = ctx.enter_context
        ones = en(nc.sbuf_tensor("ones", [128, 128], BF16))
        cosT = en(nc.sbuf_tensor("cosT", [64, TOK], F32))
        sinT = en(nc.sbuf_tensor("sinT", [64, TOK], F32))
        xnA = en(nc.sbuf_tensor("xnA", [128, ROWB], BF16))
        KTA = en(nc.sbuf_tensor("KTA", [128, ROWB], BF16))
        VTA = en(nc.sbuf_tensor("VTA", [128, ROWB], BF16))
        QTA = en(nc.sbuf_tensor("QTA", [128, ROWB], BF16))
        CTA = en(nc.sbuf_tensor("CTA", [128, ROWB], BF16))
        wbuf = en(nc.sbuf_tensor("wbuf", [128, 8192], BF16))
        xstg = en(nc.sbuf_tensor("xstg", [128, 1024], F32))
        yout = en(nc.sbuf_tensor("yout", [128, 1024], F32))
        ebuf = en(nc.sbuf_tensor("ebuf", [128, 1024], BF16))
        m_sb = en(nc.sbuf_tensor("m_sb", [128, 512], F32))
        rs_sb = en(nc.sbuf_tensor("rs_sb", [128, 512], F32))
        var_sb = en(nc.sbuf_tensor("var_sb", [128, 512], F32))
        tmp_sb = en(nc.sbuf_tensor("tmp_sb", [128, 512], F32))
        rb_sb = en(nc.sbuf_tensor("rb_sb", [1, 4], I32))

        bt_u8s = [VTA[:, 0:1024].bitcast(U8),
                  VTA[:, 13312:14336].bitcast(U8)]
        bf_f = VTA[:, 1024:5120].bitcast(F32)
        tt_f = VTA[:, 5120:9216].bitcast(F32)
        hi_i = VTA[:, 9216:13312].bitcast(I32)
        lo_f = CTA[:, 4096:8192].bitcast(F32)
        sb_f = CTA[:, 8192:16384].bitcast(F32)
        zsb_f = KTA[:, 0:8192].bitcast(F32)
        Wt_b = KTA[:, 8192:12288]
        zt_f = KTA[:, 12288:16384].bitcast(F32)
        rtv = CTA[:, 0:4096].bitcast(F32)
        sqst = ebuf
        sgst = ebuf
        zrow_z = zt_f[0:32, 0:768]
        zrow_s = zt_f[0:32, 768:1536]

        pb = [en(nc.psum_tensor(f"pb{i}", [128, 512], F32)) for i in range(8)]

        sems = {s: en(nc.semaphore("sem_" + s)) for s in
                ['g', 'w', 'x', 'c', 'T', 'V', 'A',
                 'x0', 'x1', 'w0', 'w1', 'u0', 'u1',
                 'd0', 'd1', 'd2', 'd3', 'kv0', 'kv1', 'y0', 'y1',
                 'q0', 'q1', 'pc']}

        block = en(nc.Block())

        def dap(h, off, dims):
            return bass.AP(h, off, dims)

        def wslot(par):
            return bass.AP(wbuf, par * 4096, [[8192, 128], [128, 32], [1, 128]])

        def plo(ap_):
            return bass.AP(ap_.tensor, ap_.offset, [[ap_.ap[0][0], 64], list(ap_.ap[1])])

        def phi(ap_):
            return bass.AP(ap_.tensor, ap_.offset + 64 * ap_.ap[0][0],
                           [[ap_.ap[0][0], 64], list(ap_.ap[1])])

        def slab(view, m4, nW, np_):
            return bass.AP(view.tensor, view.offset + m4,
                           [[view.ap[0][0], np_], [4, nW]])

        def outslab(view, p8, nW, np_):
            return bass.AP(view.tensor, view.offset + ARO_INV[p8],
                           [[view.ap[0][0], np_], [8, nW]])

        def xchunk(src, dc, c0):
            return dap(src, dc * 128 * TOK + c0, [[TOK, 128], [1, QTS]])

        def emit_ln(C, src, c0):
            C.w('a', 'T', C.n['T'])
            C.w('a', 'V', C.n['V'])
            for s_ in ('x0', 'x1', 'kv0', 'kv1'):
                C.w('a', s_, C.n[s_])
            t0 = C.n['T']
            a0 = C.n['A']
            lmk = [C.dma('x', xstg[:, 0:512], xchunk(src, 0, c0), sem='x0')]
            for dc in range(NDC):
                par = dc % 2
                if dc + 1 < NDC:
                    s2 = 'x' + str((dc + 1) % 2)
                    lmk.append(C.dma('x', xstg[:, ((dc + 1) % 2) * 512:((dc + 1) % 2 + 1) * 512],
                                     xchunk(src, dc + 1, c0), sem=s2))
                C.w('a', 'x' + str(par), lmk[dc])
                i1 = C.a.activation(xnA[:, dc * 512:(dc + 1) * 512],
                                    xstg[:, par * 512:(par + 1) * 512], AF.Copy)
                C.inc(i1, 'A')
                if dc >= 2:
                    C.w('a', 'T', t0 + dc - 1)
                i2 = C.a.activation(sqst[:, par * 512:(par + 1) * 512],
                                    xstg[:, par * 512:(par + 1) * 512], AF.Square)
                C.inc(i2, 'A')
                C.w('t', 'A', a0 + 2 * (dc + 1))
                C.t.matmul(pb[6][:, :], ones[:, :], xnA[:, dc * 512:(dc + 1) * 512],
                           start=(dc == 0), stop=(dc == NDC - 1), skip_group_check=True)
                i3 = C.t.matmul(pb[7][:, :], ones[:, :], sqst[:, par * 512:(par + 1) * 512],
                                start=(dc == 0), stop=(dc == NDC - 1), skip_group_check=True)
                C.inc(i3, 'T')
            C.w('v', 'T', t0 + NDC)
            C.v.tensor_scalar(m_sb[:, :], pb[6][:, :], 1.0 / D, None, ALU.mult)
            C.v.tensor_scalar(var_sb[:, :], pb[7][:, :], 1.0 / D, None, ALU.mult)
            C.v.tensor_tensor(tmp_sb[:, :], m_sb[:, :], m_sb[:, :], ALU.mult)
            C.v.tensor_tensor(var_sb[:, :], var_sb[:, :], tmp_sb[:, :], ALU.subtract)
            iv = C.v.tensor_scalar(var_sb[:, :], var_sb[:, :], EPS, None, ALU.add)
            v1 = C.inc(iv, 'V')
            C.w('a', 'V', v1)
            ia = C.a.activation(tmp_sb[:, :], var_sb[:, :], AF.Sqrt)
            a1 = C.inc(ia, 'A')
            C.w('v', 'A', a1)
            ivr = C.v.reciprocal(rs_sb[:, :], tmp_sb[:, :])
            C.inc(ivr, 'V')
            xm = []
            if PASS2:
                lmk2 = [C.dma('x', xstg[:, 0:512], xchunk(src, 0, c0), sem='x0')]
                for dc in range(NDC):
                    par = dc % 2
                    if dc + 1 < NDC:
                        if dc >= 1:
                            C.w('a', 'V', xm[dc - 1])
                        s2 = 'x' + str((dc + 1) % 2)
                        lmk2.append(C.dma('x', xstg[:, ((dc + 1) % 2) * 512:((dc + 1) % 2 + 1) * 512],
                                          xchunk(src, dc + 1, c0), sem=s2))
                    C.w('v', 'x' + str(par), lmk2[dc])
                    C.v.tensor_tensor(var_sb[:, :], xstg[:, par * 512:(par + 1) * 512],
                                      m_sb[:, :], ALU.subtract)
                    iv = C.v.tensor_tensor(xnA[:, dc * 512:(dc + 1) * 512], var_sb[:, :],
                                           rs_sb[:, :], ALU.mult)
                    xm.append(C.inc(iv, 'V'))
            else:
                for dc in range(NDC):
                    C.v.tensor_tensor(var_sb[:, :], xnA[:, dc * 512:(dc + 1) * 512],
                                      m_sb[:, :], ALU.subtract)
                    iv = C.v.tensor_tensor(xnA[:, dc * 512:(dc + 1) * 512], var_sb[:, :],
                                           rs_sb[:, :], ALU.mult)
                    xm.append(C.inc(iv, 'V'))
                    if dc % 2 == 1:
                        C.w('t', 'V', xm[-1])
                        itf = C.t.matmul(pb[6][:, 0:128], ones[:, :], ones[:, 0:128],
                                         start=True, stop=True, skip_group_check=True)
                        C.inc(itf, 'T')
            return C.n['V']

        def emit_rope(C, psum, out_bf, c0):
            cs = cosT[:, c0:c0 + 512]
            sn = sinT[:, c0:c0 + 512]
            lo_ps = psum[0:64, :]
            hi_ps = psum[64:128, :]
            ra = rtv[0:64, 0:512]
            rb = rtv[0:64, 512:1024]
            C.v.tensor_tensor(ra, lo_ps, cs, ALU.mult)
            C.v.tensor_tensor(rb, hi_ps, sn, ALU.mult)
            C.v.tensor_tensor(plo(out_bf), ra, rb, ALU.subtract)
            C.v.tensor_tensor(ra, hi_ps, cs, ALU.mult)
            C.v.tensor_tensor(rb, lo_ps, sn, ALU.mult)
            return C.v.tensor_tensor(phi(out_bf), ra, rb, ALU.add)

        def emit_dequant_prologue(C):
            for (srcw, dstw) in ((wq_a, bq_a), (wo_a, bo_a)):
                C.dma('g', dstw.ap(), srcw.ap(), sem='pc')
            C.dma('x', ones[:, :], onesI.ap(), sem='y0')
            C.dma('x', cosT[:, :], cosI.ap(), sem='y0')
            C.dma('x', sinT[:, :], sinI.ap(), sem='y0')
            C.dma('x', rb_sb[:, :], rbI.ap(), sem='y0')
            C.w('t', 'y0', C.n['y0'])
            C.w('v', 'y0', C.n['y0'])

            def unpack_stage(np_, nbytes):
                C.v.tensor_scalar(tt_f[0:np_, 0:nbytes], bf_f[0:np_, 0:nbytes], 0.0625,
                                  0.46875, ALU.mult, ALU.subtract)
                C.v.tensor_copy(hi_i[0:np_, 0:nbytes], tt_f[0:np_, 0:nbytes])
                C.v.scalar_tensor_tensor(lo_f[0:np_, 0:nbytes], hi_i[0:np_, 0:nbytes],
                                         -16.0, bf_f[0:np_, 0:nbytes], ALU.mult, ALU.add)

            chunks = []
            cst = {'i': 0, 'marks': []}

            def bt_next():
                ci = cst['i']
                cst['i'] += 1
                par = ci % 2
                if ci >= 2:
                    C.w('g', 'V', cst['marks'][ci - 2])
                return par

            def stage_z(first, half, zbI_, scI_, zsD):
                c0 = half * 768
                par = bt_next()
                bt_u8 = bt_u8s[par]
                gl = C.dma('g', bt_u8[0:32, 0:384], dap(zbI_, half * 384, [[768, 32], [1, 384]]),
                           sem='d' + str(par))
                xl = C.dma('x', zrow_s, dap(scI_, c0, [[1536, 32], [1, 768]]))
                C.w('v', 'd' + str(par), gl)
                if not first:
                    C.w('v', 'x', C.n['x'])
                ic1 = C.v.tensor_copy(bf_f[0:32, 0:384], bt_u8[0:32, 0:384])
                cst['marks'].append(C.inc(ic1, 'V'))
                unpack_stage(32, 384)
                for m4 in range(4):
                    for lohi in range(2):
                        p8 = 2 * m4 + lohi
                        srct = lo_f if lohi == 0 else hi_i
                        C.v.tensor_copy(outslab(zrow_z, p8, 96, 32), slab(srct, m4, 96, 32))
                C.w('v', 'x', xl)
                iv = C.v.tensor_tensor(zrow_z, zrow_z, zrow_s, ALU.mult)
                vz = C.inc(iv, 'V')
                C.w('a', 'V', vz)
                C.dma('x', dap(zsD, c0, [[1536, 32], [1, 768]]), zrow_z)

            for zi_, zpk in enumerate(((gzb, gsc, zs_g), (uzb, usc, zs_u))):
                for h_ in range(2):
                    chunks.append(lambda f_=(zi_ == 0 and h_ == 0), h_=h_, zpk=zpk:
                                  stage_z(f_, h_, *zpk))

            def down_zeros():
                par = bt_next()
                bt_u8 = bt_u8s[par]
                gl = C.dma('g', bt_u8[0:16, 0:2048], dzb.ap(), sem='d' + str(par))
                C.w('v', 'd' + str(par), gl)
                C.w('v', 'x', C.n['x'])
                ic1 = C.v.tensor_copy(bf_f[0:16, 0:2048], bt_u8[0:16, 0:2048])
                cst['marks'].append(C.inc(ic1, 'V'))
                unpack_stage(16, 2048)
                for m4 in range(4):
                    for lohi in range(2):
                        p8 = 2 * m4 + lohi
                        srct = lo_f if lohi == 0 else hi_i
                        C.v.tensor_copy(outslab(zsb_f[0:16, 0:4096], p8, 512, 16),
                                        slab(srct, m4, 512, 16))
                xl = C.dma('x', sb_f[0:16, 0:4096], dsc.ap())
                C.w('v', 'x', xl)
                iv = C.v.tensor_tensor(sb_f[0:16, 0:4096], zsb_f[0:16, 0:4096],
                                       sb_f[0:16, 0:4096], ALU.mult)
                vz = C.inc(iv, 'V')
                C.w('a', 'V', vz)
                C.zsd_mark = C.dma('x', zs_d.ap(), sb_f[0:16, 0:4096])

            chunks.append(down_zeros)

            def gu_chunk(mi, dc, qbI_, scI_, zsD):
                if mi == 0 and dc == 0:
                    C.w('a', 'x', C.zsd_mark)
                par = bt_next()
                bt_u8 = bt_u8s[par]
                gl = C.dma('g', bt_u8[:, 0:768],
                           dap(qbI_, dc * 128 * 768, [[768, 128], [1, 768]]),
                           sem='d' + str(par))
                C.w('a', 'V', C.n['V'])
                x1 = C.dma('x', sb_f[:, 0:1536], dap(scI_, dc * 1536, [[0, 128], [1, 1536]]))
                x2 = C.dma('x', zsb_f[:, 0:1536], dap(zsD, dc * 1536, [[0, 128], [1, 1536]]))
                C.w('v', 'd' + str(par), gl)
                C.w('v', 'x', x2)
                C.w('v', 'w', C.n['w'])
                ic1 = C.v.tensor_copy(bf_f[:, 0:768], bt_u8[:, 0:768])
                cst['marks'].append(C.inc(ic1, 'V'))
                unpack_stage(128, 768)
                ivv = None
                for m4 in range(4):
                    for lohi in range(2):
                        p8 = 2 * m4 + lohi
                        srct = lo_f if lohi == 0 else hi_i
                        C.v.tensor_tensor(tt_f[:, 0:192], slab(srct, m4, 192, 128),
                                          outslab(sb_f[:, 0:1536], p8, 192, 128), ALU.mult)
                        ivv = C.v.tensor_tensor(outslab(Wt_b[:, 0:1536], p8, 192, 128),
                                                tt_f[:, 0:192],
                                                outslab(zsb_f[:, 0:1536], p8, 192, 128),
                                                ALU.subtract)
                vW = C.inc(ivv, 'V')
                C.w('s', 'V', vW)
                C.dma('w',
                      dap(deqp, (mi * 12) * 128 * 4096 + dc * 128,
                          [[4096, 128], [128 * 4096, 12], [1, 128]]),
                      bass.AP(Wt_b.tensor, Wt_b.offset,
                              [[ROWB, 128], [128, 12], [1, 128]]))

            for mi_, gpk in enumerate(((gqb, gsc, zs_g), (uqb, usc, zs_u))):
                for dc_ in range(NDC):
                    chunks.append(lambda mi_=mi_, dc_=dc_, gpk=gpk: gu_chunk(mi_, dc_, *gpk))

            def down_chunk(l):
                par = bt_next()
                bt_u8 = bt_u8s[par]
                gl = C.dma('g', bt_u8[:, 0:2048],
                           dap(dqb, l * 128 * 2048, [[2048, 128], [1, 2048]]),
                           sem='d' + str(par))
                C.w('a', 'V', C.n['V'])
                x1 = C.dma('x', sb_f[:, 0:4096], dap(dsc, l * 4096, [[0, 128], [1, 4096]]))
                x2 = C.dma('x', zsb_f[:, 0:4096], dap(zs_d, l * 4096, [[0, 128], [1, 4096]]))
                C.w('v', 'd' + str(par), gl)
                C.w('v', 'x', x2)
                C.w('v', 'w', C.n['w'])
                ic1 = C.v.tensor_copy(bf_f[:, 0:2048], bt_u8[:, 0:2048])
                cst['marks'].append(C.inc(ic1, 'V'))
                unpack_stage(128, 2048)
                ivv = None
                for m4 in range(4):
                    for lohi in range(2):
                        p8 = 2 * m4 + lohi
                        srct = lo_f if lohi == 0 else hi_i
                        C.v.tensor_tensor(tt_f[:, 0:512], slab(srct, m4, 512, 128),
                                          outslab(sb_f[:, 0:4096], p8, 512, 128), ALU.mult)
                        ivv = C.v.tensor_tensor(outslab(Wt_b[:, 0:4096], p8, 512, 128),
                                                tt_f[:, 0:512],
                                                outslab(zsb_f[:, 0:4096], p8, 512, 128),
                                                ALU.subtract)
                vW = C.inc(ivv, 'V')
                C.w('s', 'V', vW)
                C.dma('w', dap(deqp, (24 + l) * 128 * 4096, [[4096, 128], [1, 4096]]),
                      Wt_b[:, 0:4096])

            for l_ in range(LI):
                chunks.append(lambda l_=l_: down_chunk(l_))

            def fire_deq_ag():
                C.wait_all_dma('g')
                icc = C.g.collective_compute(
                    "AllGather", ALU.bypass, replica_groups=[list(range(NCORES))],
                    ins=[deqp.ap().opt()], outs=[deqfull.ap().opt()])
                return C.inc(icc, 'c')
            return chunks, fire_deq_ag

        def emit_attn(C, is_a, src, dst, fire_after_kv=None, deq_chunks=None):
            wq_, wk_, wv_, wo_ = (bq_a, wk_a, wv_a, bo_a) if is_a else \
                                 (bq_b, bk_b, bv_b, bo_b)
            dq = deq_chunks if deq_chunks is not None else []

            def drain(nch):
                while nch > 0 and dq:
                    dq.pop(0)()
                    nch -= 1
            if not is_a:
                C.w('s', 'pc', 6 * 16)
            ev = 'V' if is_a else 'A'
            kvp_marks = []
            for tile in range(2):
                vln = emit_ln(C, src, tile * QTS)
                kq_loads = []
                kq_marks = []

                def issue_k(kvc):
                    p_ = kvc % 2
                    ke = 'g' if is_a else 's'
                    if kvc >= 2:
                        C.w(ke, 'T', kq_marks[kvc - 2])
                    else:
                        C.w(ke, 'T', C.n['T'])
                    kq_loads.append(
                        C.dma('g' if is_a else 'w', wslot(p_),
                              dap(wk_, kvc * 128, [[1024, 128], [128 * 1024, 32], [1, 128]]),
                              sem='q' + str(p_)))

                issue_k(0)
                for kvc in range(8):
                    par = kvc % 2
                    if kvc + 1 < 8:
                        issue_k(kvc + 1)
                    C.w('t', 'q' + str(par), kq_loads[kvc])
                    C.w('t', 'V', vln)
                    C.w('t', ev, C.n[ev])
                    it = None
                    for dc in range(NDC):
                        it = C.t.matmul(pb[4 + par][:, :],
                                        wbuf[:, par * 4096 + dc * 128:par * 4096 + (dc + 1) * 128],
                                        xnA[:, dc * 512:(dc + 1) * 512],
                                        start=(dc == 0), stop=(dc == NDC - 1))
                    tK = C.inc(it, 'T')
                    kq_marks.append(tK)
                    eb = ebuf[:, par * 512:(par + 1) * 512]
                    if len(kvp_marks) >= 2:
                        C.w('v' if is_a else 'a', 'kv' + str(par), kvp_marks[-2])
                    if is_a:
                        C.w('v', 'T', tK)
                        iv = emit_rope(C, pb[4 + par], eb, tile * QTS)
                        emark = C.inc(iv, 'V')
                    else:
                        C.w('a', 'T', tK)
                        ia = C.a.activation(eb, pb[4 + par][:, :], AF.Copy)
                        emark = C.inc(ia, 'A')
                    C.w('a', ev, emark)
                    kvp_marks.append(
                        C.dma('x', dap(kvp, kvc * 128 * 1024 + tile * QTS,
                                       [[1024, 128], [1, QTS]]), eb, sem='kv' + str(par)))
                    drain(3)
                for kvh in range(2):
                    C.w('g' if is_a else 's', 'T', C.n['T'])
                    gl = C.dma('g' if is_a else 'w', bass.AP(QTA, 0, [[ROWB, 128], [512, 32], [1, 512]]),
                               dap(wv_, kvh * 512, [[1024, 128], [128 * 1024, 32], [1, 512]]),
                               sem='q0')
                    for tc in range(4):
                        par = tc % 2
                        C.w('t', 'q0', gl)
                        C.w('t', 'A', C.n['A'])
                        it = None
                        for dc in range(NDC):
                            it = C.t.matmul(pb[4 + par][:, :],
                                            xnA[:, dc * 512 + tc * 128:dc * 512 + (tc + 1) * 128],
                                            QTA[:, dc * 512:(dc + 1) * 512],
                                            start=(dc == 0), stop=(dc == NDC - 1))
                        tV = C.inc(it, 'T')
                        C.w('a', 'T', tV)
                        eb = ebuf[:, par * 512:(par + 1) * 512]
                        if len(kvp_marks) >= 2:
                            C.w('a', 'kv' + str(par), kvp_marks[-2])
                        ia = C.a.activation(eb, pb[4 + par][:, :], AF.Copy)
                        aV = C.inc(ia, 'A')
                        kvp_marks.append(
                            C.dma('x', dap(kvp, (1024 + tile * QTS + tc * 128) * 1024 + kvh * 512,
                                           [[1024, 128], [1, 512]]), eb, sem='kv' + str(par)))
                        drain(2)
            drain(10 ** 9)
            C.wait_all_dma('g')
            icc = C.g.collective_compute(
                "AllGather", ALU.bypass, replica_groups=[list(range(NCORES))],
                ins=[kvp.ap().opt()], outs=[kvfull.ap().opt()])
            ccv = C.inc(icc, 'c')
            C.w('g', 'c', ccv)
            C.w('g', 'T', C.n['T'])
            C.w('g', 'V', C.n['V'])
            for j, dstrow in enumerate((0, 2048, 1024, 3072)):
                if C.k == 'g':
                    C.roff = C.e.value_load(rb_sb[0:1, j:j + 1])
                    i = C.e.dma_start(out=dap(kvpair, dstrow * 1024, [[1024, 1024], [1, 1024]]),
                                      in_=kvfull.ap()[bass.ds(C.roff, 1024), :])
                else:
                    i = None
                C.inc(i, 'kv0' if j < 2 else 'kv1', 16)
            C.w('g', 'kv0', C.n['kv0'])
            C.w('g', 'kv1', C.n['kv1'])
            for g8 in range(8):
                for half in range(2):
                    C.dma('g',
                          KTA[:, g8 * 2048 + half * 1024:g8 * 2048 + (half + 1) * 1024],
                          dap(kvpair, (half * 2048 + g8 * 128) * 1024,
                              [[1024, 128], [1, 1024]]))
                    C.dma('g',
                          bass.AP(VTA, g8 * 2048 + half * 1024,
                                  [[ROWB, 128], [128, 8], [1, 128]]),
                          dap(kvpair, (half * 2048 + 1024) * 1024 + g8 * 128,
                              [[1024, 128], [128 * 1024, 8], [1, 128]]))
            kv_ready = C.n['g']
            if fire_after_kv is not None:
                C.deq_ready = fire_after_kv()

            if is_a:
                C.w('s', 'pc', 2 * 16)
            for qt in range(2):
                vln = emit_ln(C, src, qt * QTS)
                qp_loads = []
                qp_marks = []

                def issue_q(oc):
                    p_ = oc % 2
                    if oc >= 2:
                        C.w('s', 'T', qp_marks[oc - 2])
                    else:
                        C.w('s', 'T', C.n['T'])
                    qp_loads.append(
                        C.dma('w', wslot(p_),
                              dap(wq_, oc * 128, [[4096, 128], [128 * 4096, 32], [1, 128]]),
                              sem='q' + str(p_)))

                issue_q(0)
                for oc in range(NDC):
                    par = oc % 2
                    if oc + 1 < NDC:
                        issue_q(oc + 1)
                    C.w('t', 'q' + str(par), qp_loads[oc])
                    C.w('t', 'V', vln)
                    C.w('t', ev, C.n[ev])
                    it = None
                    for dc in range(NDC):
                        it = C.t.matmul(pb[4 + par][:, :],
                                        wbuf[:, par * 4096 + dc * 128:par * 4096 + (dc + 1) * 128],
                                        xnA[:, dc * 512:(dc + 1) * 512],
                                        start=(dc == 0), stop=(dc == NDC - 1))
                    tQ = C.inc(it, 'T')
                    qp_marks.append(tQ)
                    qsl = QTA[:, oc * 512:(oc + 1) * 512]
                    if is_a:
                        C.w('v', 'T', tQ)
                        iv = emit_rope(C, pb[4 + par], qsl, qt * QTS)
                        C.inc(iv, 'V')
                    else:
                        C.w('a', 'T', tQ)
                        ia = C.a.activation(qsl, pb[4 + par][:, :], AF.Copy)
                        C.inc(ia, 'A')
                tq_done = C.n['T']
                if is_a:
                    C.w('a', 'T', tq_done)
                    C.w('a', 'V', C.n['V'])
                    mload = C.dma('x', bass.AP(xnA, 8192, [[ROWB, 128], [512, 16], [1, 512]]),
                                  dap(maskI, qt * 16 * 128 * 512,
                                      [[512, 128], [128 * 512, 16], [1, 512]]))
                else:
                    mload = 0
                C.w('t', 'g', kv_ready)
                nkc = 12 if (is_a and qt == 0) else 16
                for head in range(32):
                    g8 = head // 4
                    a_exp0 = C.n['A']
                    v_msk0 = C.n['V']
                    for kc in range(nkc):
                        par = kc % 2
                        C.w('t', 'A', a_exp0 + kc - 1 if kc >= 2 else a_exp0)
                        it = C.t.matmul(pb[par][:, :],
                                        KTA[:, g8 * 2048 + kc * 128:g8 * 2048 + (kc + 1) * 128],
                                        QTA[:, head * 512:(head + 1) * 512],
                                        start=True, stop=True)
                        ts_ = C.inc(it, 'T')
                        C.w('a', 'T', ts_)
                        ia = C.a.activation(xnA[:, kc * 512:(kc + 1) * 512], pb[par][:, :],
                                            AF.Exp, scale=SCALE_QK)
                        C.inc(ia, 'A')
                        if is_a:
                            C.w('v', 'A', C.n['A'])
                            if head == 0 and kc == 0:
                                C.w('v', 'x', mload)
                            iv = C.v.tensor_tensor(xnA[:, kc * 512:(kc + 1) * 512],
                                                   xnA[:, kc * 512:(kc + 1) * 512],
                                                   xnA[:, 8192 + kc * 512:8192 + (kc + 1) * 512],
                                                   ALU.mult)
                            C.inc(iv, 'V')
                    if is_a:
                        C.w('t', 'V', v_msk0 + nkc)
                    else:
                        C.w('t', 'A', a_exp0 + nkc)
                    C.w('t', 'V', C.n['V'])
                    it = None
                    for kc in range(nkc):
                        it = C.t.matmul(pb[2][:, :], ones[:, :],
                                        xnA[:, kc * 512:(kc + 1) * 512],
                                        start=(kc == 0), stop=(kc == nkc - 1))
                    tsum = C.inc(it, 'T')
                    C.w('v', 'T', tsum)
                    iv = C.v.reciprocal(rs_sb[:, :], pb[2][:, :])
                    C.inc(iv, 'V')
                    it = None
                    for kc in range(nkc):
                        it = C.t.matmul(pb[3][:, :],
                                        VTA[:, g8 * 2048 + kc * 128:g8 * 2048 + (kc + 1) * 128],
                                        xnA[:, kc * 512:(kc + 1) * 512],
                                        start=(kc == 0), stop=(kc == nkc - 1))
                    tctx = C.inc(it, 'T')
                    C.w('v', 'T', tctx)
                    iv = C.v.tensor_tensor(CTA[:, head * 512:(head + 1) * 512], pb[3][:, :],
                                           rs_sb[:, :], ALU.mult)
                    C.inc(iv, 'V')
                vctx_done = C.n['V']
                res_marks = []
                wo_loads = []
                wo_marks = []

                def issue_wo(hc):
                    p_ = hc % 2
                    if hc >= 2:
                        C.w('s', 'T', wo_marks[hc - 2])
                    else:
                        C.w('s', 'T', C.n['T'])
                    wo_loads.append(
                        C.dma('w', wslot(p_),
                              dap(wo_, hc * 128, [[4096, 128], [128 * 4096, 32], [1, 128]]),
                              sem='q' + str(p_)))

                issue_wo(0)
                for hc in range(NDC):
                    par = hc % 2
                    if hc + 1 < NDC:
                        issue_wo(hc + 1)
                    if len(res_marks) >= 2:
                        C.w('a', 'V', res_marks[-2][0])
                    xl = C.dma('x', xstg[:, par * 512:(par + 1) * 512],
                               xchunk(src, hc, qt * QTS), sem='x' + str(par))
                    C.w('t', 'q' + str(par), wo_loads[hc])
                    C.w('t', 'V', vctx_done)
                    C.w('t', 'V', C.n['V'])
                    it = None
                    for oc in range(NDC):
                        it = C.t.matmul(pb[4 + par][:, :],
                                        wbuf[:, par * 4096 + oc * 128:par * 4096 + (oc + 1) * 128],
                                        CTA[:, oc * 512:(oc + 1) * 512],
                                        start=(oc == 0), stop=(oc == NDC - 1))
                    tWo = C.inc(it, 'T')
                    wo_marks.append(tWo)
                    C.w('v', 'T', tWo)
                    C.w('v', 'x' + str(par), xl)
                    if len(res_marks) >= 2:
                        C.w('v', 'y' + str(par), res_marks[-2][1])
                    iv = C.v.tensor_tensor(yout[:, par * 512:(par + 1) * 512], pb[4 + par][:, :],
                                           xstg[:, par * 512:(par + 1) * 512], ALU.add)
                    vres = C.inc(iv, 'V')
                    C.w('a', 'V', vres)
                    wm = C.dma('x', xchunk(dst, hc, qt * QTS),
                               yout[:, par * 512:(par + 1) * 512], sem='y' + str(par))
                    res_marks.append((vres, wm))
            if is_a:
                for (srcw, dstw) in ((wq_b, bq_b), (wk_b, bk_b), (wv_b, bv_b), (wo_b, bo_b)):
                    C.dma('g', dstw.ap(), srcw.ap(), sem='pc')

        def emit_ffn(C, src, dst, deq_ready):
            def h_view(ic):
                if ic < 32:
                    return QTA[:, ic * 512:(ic + 1) * 512]
                if ic < 64:
                    return CTA[:, (ic - 32) * 512:(ic - 31) * 512]
                return KTA[:, (ic - 64) * 512:(ic - 63) * 512]

            def gblk(mi, ic):
                r_, l_ = ic // LI, ic % LI
                return (r_ * 36 + mi * 12 + l_) * 128 * 4096

            C.w('s', 'c', deq_ready)
            for tt in range(2):
                vln = emit_ln(C, src, tt * QTS)
                gu_loads = []
                gu_marks = []

                def issue_gu(ic):
                    par = ic % 2
                    if ic >= 2:
                        C.w('s', 'T', gu_marks[ic - 2])
                    else:
                        C.w('s', 'T', C.n['T'])
                        C.w('s', 'V', C.n['V'])
                    C.dma('w', wbuf[:, par * 4096:(par + 1) * 4096],
                          dap(deqfull, gblk(0, ic), [[4096, 128], [1, 4096]]), sem='w' + str(par))
                    gu_loads.append(
                        (C.n['w' + str(par)],
                         C.dma('w', VTA[:, par * 4096:(par + 1) * 4096],
                               dap(deqfull, gblk(1, ic), [[4096, 128], [1, 4096]]),
                               sem='u' + str(par))))

                issue_gu(0)
                for ic in range(ICH):
                    par = ic % 2
                    if ic + 1 < ICH:
                        issue_gu(ic + 1)
                    C.w('t', 'w' + str(par), gu_loads[ic][0])
                    C.w('t', 'u' + str(par), gu_loads[ic][1])
                    C.w('t', 'V', vln)
                    C.w('t', 'A', C.n['A'])
                    C.w('t', 'V', C.n['V'])
                    for dc in range(NDC):
                        C.t.matmul(pb[0][:, :],
                                   wbuf[:, par * 4096 + dc * 128:par * 4096 + (dc + 1) * 128],
                                   xnA[:, dc * 512:(dc + 1) * 512],
                                   start=(dc == 0), stop=(dc == NDC - 1))
                    it = None
                    for dc in range(NDC):
                        it = C.t.matmul(pb[1][:, :],
                                        VTA[:, par * 4096 + dc * 128:par * 4096 + (dc + 1) * 128],
                                        xnA[:, dc * 512:(dc + 1) * 512],
                                        start=(dc == 0), stop=(dc == NDC - 1))
                    tg = C.inc(it, 'T')
                    gu_marks.append(tg)
                    C.w('a', 'T', tg)
                    ia = C.a.activation(sgst[:, par * 512:(par + 1) * 512], pb[0][:, :], AF.Silu)
                    C.inc(ia, 'A')
                    C.w('v', 'A', C.n['A'])
                    iv = C.v.tensor_tensor(h_view(ic), sgst[:, par * 512:(par + 1) * 512],
                                           pb[1][:, :], ALU.mult)
                    C.inc(iv, 'V')
                h_done = C.n['V']
                if ffn_mode == 'nodown':
                    res_marks = []
                    for hc in range(NDC):
                        par = hc % 2
                        if len(res_marks) >= 2:
                            C.w('a', 'V', res_marks[-2][0])
                        xl = C.dma('x', xstg[:, par * 512:(par + 1) * 512],
                                   xchunk(src, hc, tt * QTS), sem='x' + str(par))
                        C.w('v', 'x' + str(par), xl)
                        if len(res_marks) >= 2:
                            C.w('v', 'y' + str(par), res_marks[-2][1])
                        iv = C.v.tensor_scalar(yout[:, par * 512:(par + 1) * 512],
                                               xstg[:, par * 512:(par + 1) * 512], 1.0, None,
                                               ALU.mult)
                        vres = C.inc(iv, 'V')
                        C.w('a', 'V', vres)
                        wm = C.dma('x', xchunk(dst, hc, tt * QTS),
                                   yout[:, par * 512:(par + 1) * 512], sem='y' + str(par))
                        res_marks.append((vres, wm))
                    continue
                d_loads = []
                d_marks = []

                def issue_dn(k):
                    dcp_, ic_ = k // ICH, k % ICH
                    sl = k % 4
                    if k >= 4:
                        C.w('s', 'T', d_marks[k - 4])
                    r_, l_ = ic_ // LI, ic_ % LI
                    d_loads.append(
                        C.dma('w', VTA[:, 8192 + sl * 1024:8192 + (sl + 1) * 1024],
                              dap(deqfull, (r_ * 36 + 24 + l_) * 128 * 4096 + dcp_ * 1024,
                                  [[4096, 128], [1, 1024]]), sem='d' + str(sl)))

                issue_dn(0)
                issue_dn(1)
                for dcp in range(4):
                    for ic in range(ICH):
                        k = dcp * ICH + ic
                        sl = k % 4
                        if k + 2 < 4 * ICH:
                            issue_dn(k + 2)
                        C.w('t', 'd' + str(sl), d_loads[k])
                        if k == 0:
                            C.w('t', 'V', h_done)
                        it = None
                        for j in range(8):
                            it = C.t.matmul(
                                pb[j][:, :],
                                VTA[:, 8192 + sl * 1024 + j * 128:8192 + sl * 1024 + (j + 1) * 128],
                                h_view(ic),
                                start=(ic == 0), stop=(ic == ICH - 1), skip_group_check=True)
                        d_marks.append(C.inc(it, 'T'))
                    tdown = C.n['T']
                    res_marks = []
                    for j in range(8):
                        par = j % 2
                        hc = dcp * 8 + j
                        if len(res_marks) >= 2:
                            C.w('a', 'V', res_marks[-2][0])
                        xl = C.dma('x', xstg[:, par * 512:(par + 1) * 512],
                                   xchunk(src, hc, tt * QTS), sem='x' + str(par))
                        C.w('v', 'T', tdown)
                        C.w('v', 'x' + str(par), xl)
                        if len(res_marks) >= 2:
                            C.w('v', 'y' + str(par), res_marks[-2][1])
                        iv = C.v.tensor_tensor(yout[:, par * 512:(par + 1) * 512],
                                               pb[j][:, :],
                                               xstg[:, par * 512:(par + 1) * 512], ALU.add)
                        vres = C.inc(iv, 'V')
                        C.w('a', 'V', vres)
                        wm = C.dma('x', xchunk(dst, hc, tt * QTS),
                                   yout[:, par * 512:(par + 1) * 512], sem='y' + str(par))
                        res_marks.append((vres, wm))
                    C.w('t', 'V', C.n['V'])

        def program(C):
            deq_chunks, fire_deq_ag = emit_dequant_prologue(C)
            C.deq_ready = None
            kinds = ['attn_a', 'ffn', 'attn_b', 'ffn']
            for sub in range(n_sub):
                src = xT if sub == 0 else xstate
                dst = yT if sub == n_sub - 1 else xstate
                k = kinds[sub]
                if k == 'ffn':
                    if C.deq_ready is None:
                        C.deq_ready = fire_deq_ag()
                    emit_ffn(C, src, dst, C.deq_ready)
                else:
                    emit_attn(C, k == 'attn_a', src, dst,
                              fire_deq_ag if C.deq_ready is None else None,
                              deq_chunks=deq_chunks if sub == 0 else None)
            if n_sub == 0:
                for dc in range(NDC):
                    for hh in range(2):
                        l = C.dma('g', xstg[:, hh * 512:(hh + 1) * 512],
                                  xchunk(xT, dc, hh * 512))
                        C.w('g', 'g', l)
                        C.dma('g', xchunk(yT, dc, hh * 512), xstg[:, hh * 512:(hh + 1) * 512])
            C.wait_all_dma('g')

        counts = {}

        def run_pass(kind, eng):
            c = C(kind, eng, sems)
            program(c)
            counts[kind] = dict(c.n)
            counts['__trace__'] = c.trace

        block.tensor(lambda e: run_pass('t', e))
        block.vector(lambda e: run_pass('v', e))
        block.scalar(lambda e: run_pass('a', e))
        block.gpsimd(lambda e: run_pass('g', e))
        block.sync(lambda e: run_pass('s', e))

        base = counts['t']
        for k2, v2 in counts.items():
            if k2 == '__trace__':
                continue
            assert v2 == base, f"sem count mismatch {k2}: {v2} vs {base}"
        _simulate_deadlock(counts['__trace__'])

    return nc


def _host_inputs(ins):
    x = np.asarray(ins["L0_attn_a_hidden_states"], np.float32)
    wq_a = np.ascontiguousarray(np.asarray(ins["L0_attn_a_Wq"], np.float32).T)
    wk_a = np.ascontiguousarray(np.asarray(ins["L0_attn_a_Wk"], np.float32).T)
    wv_a = np.ascontiguousarray(np.asarray(ins["L0_attn_a_Wv"], np.float32).T)
    wo_a = np.ascontiguousarray(np.asarray(ins["L0_attn_a_Wo"], np.float32).T)
    wq_b = np.ascontiguousarray(np.asarray(ins["L0_attn_b_wq"], np.float32))
    wk_b = np.ascontiguousarray(np.asarray(ins["L0_attn_b_wk"], np.float32))
    wv_b = np.ascontiguousarray(np.asarray(ins["L0_attn_b_wv"], np.float32))
    wo_b = np.ascontiguousarray(np.asarray(ins["L0_attn_b_wo"], np.float32))

    def padw(a, cols):
        out = np.zeros((a.shape[0], cols), a.dtype)
        out[:, :a.shape[1]] = a
        return out

    gq_p = padw(np.asarray(ins["L0_ffn_gate_qweight"], np.int32), 1536)
    uq_p = padw(np.asarray(ins["L0_ffn_up_qweight"], np.int32), 1536)
    gz_p = padw(np.asarray(ins["L0_ffn_gate_qzeros"], np.int32), 1536)
    uz_p = padw(np.asarray(ins["L0_ffn_up_qzeros"], np.int32), 1536)
    gs_p = padw(np.asarray(ins["L0_ffn_gate_scales"], np.float32), 12288)
    us_p = padw(np.asarray(ins["L0_ffn_up_scales"], np.float32), 12288)
    dq_p = np.zeros((12288, 512), np.int32)
    dq_p[:11008] = np.asarray(ins["L0_ffn_down_qweight"], np.int32)
    dz_p = np.zeros((128, 512), np.int32)
    dz_p[:86] = np.asarray(ins["L0_ffn_down_qzeros"], np.int32)
    ds_p = np.zeros((128, 4096), np.float32)
    ds_p[:86] = np.asarray(ins["L0_ffn_down_scales"], np.float32)

    pos = np.arange(2048, dtype=np.float32)
    freqs = np.exp(-np.arange(0, 128, 2, dtype=np.float32) * (np.log(10000.0) / 128.0))
    ang = pos[:, None] * freqs[None, :]
    cosf = np.ascontiguousarray(np.cos(ang).astype(np.float32).T)
    sinf = np.ascontiguousarray(np.sin(ang).astype(np.float32).T)
    onesc = np.ones((128, 128), dtype=ml_dtypes.bfloat16)

    in_maps = []
    for c in range(NCORES):
        b, h = c // 2, c % 2
        p0 = 1024 * h
        xTc = np.ascontiguousarray(x[b, p0:p0 + 1024, :].T)
        masks = np.zeros((2, 16, 128, 512), dtype=ml_dtypes.bfloat16)
        for qt in range(2):
            qpos = p0 + qt * 512 + np.arange(512)
            for kc in range(16):
                kpos = kc * 128 + np.arange(128)
                masks[qt, kc] = (kpos[:, None] <= qpos[None, :]).astype(ml_dtypes.bfloat16)
        r = c
        dz16 = np.zeros((16, 512), np.int32)
        dz16[:12] = dz_p[12 * r:12 * r + 12]
        ds16 = np.zeros((16, 4096), np.float32)
        ds16[:12] = ds_p[12 * r:12 * r + 12]
        in_maps.append({
            "xT": xTc,
            "wq_a": wq_a, "wk_a": wk_a, "wv_a": wv_a, "wo_a": wo_a,
            "wq_b": wq_b, "wk_b": wk_b, "wv_b": wv_b, "wo_b": wo_b,
            "gqb": np.ascontiguousarray(gq_p[:, 192 * r:192 * (r + 1)]).view(np.uint8),
            "uqb": np.ascontiguousarray(uq_p[:, 192 * r:192 * (r + 1)]).view(np.uint8),
            "dqb": np.ascontiguousarray(dq_p[1536 * r:1536 * (r + 1), :]).view(np.uint8),
            "gsc": np.ascontiguousarray(gs_p[:, 1536 * r:1536 * (r + 1)]),
            "usc": np.ascontiguousarray(us_p[:, 1536 * r:1536 * (r + 1)]),
            "dsc": ds16,
            "gzb": np.ascontiguousarray(gz_p[:, 192 * r:192 * (r + 1)]).view(np.uint8),
            "uzb": np.ascontiguousarray(uz_p[:, 192 * r:192 * (r + 1)]).view(np.uint8),
            "dzb": dz16.view(np.uint8),
            "cosI": np.ascontiguousarray(cosf[:, p0:p0 + 1024]),
            "sinI": np.ascontiguousarray(sinf[:, p0:p0 + 1024]),
            "maskI": masks,
            "onesI": onesc,
            "rbI": np.array([[(c & ~1) * 2048 + off for off in (0, 2048, 1024, 3072)]],
                            np.int32),
        })
    return in_maps


def kernel_run(n_sub=4, dump_deq=False, ffn_mode='full', **inputs):
    in_maps = _host_inputs(inputs)
    nc = build(n_sub, dump_deq, ffn_mode)
    res = run_bass_kernel_spmd(nc, in_maps, core_ids=list(range(NCORES)))
    out = np.zeros((4, 2048, 4096), np.float32)
    for c in range(NCORES):
        b, h = c // 2, c % 2
        out[b, 1024 * h:1024 * (h + 1), :] = res.results[c]["yT"].T
    return out


def kernel(**inputs):
    return kernel_run(4, **inputs)
